# version 13
# baseline (speedup 1.0000x reference)
import numpy as np
import concourse.bass as bass
import concourse.mybir as mybir
from concourse.bass_utils import run_bass_kernel_spmd
from contextlib import ExitStack

F32 = mybir.dt.float32
BF16 = mybir.dt.bfloat16
AF = mybir.ActivationFunctionType
ALU = mybir.AluOpType
AX = mybir.AxisListType

NTOK = 2048
TT = 16
D = 1024
DC = 8
TAILC = 64
K_SWEEPS = 8
NF = 65536.0
EPS_LN = 1e-5
EPS_RMS = 1e-6
ETA_MIN = 1e-6
CLAMP = 20.0


class Prog:
    ENG = ("tensor", "vector", "scalar", "sync", "gpsimd")

    def __init__(self):
        self.ops = []

    def add(self, engine, fn, deps=(), dma=False):
        assert engine in self.ENG
        ds = sorted({d for d in deps if d is not None})
        self.ops.append({"e": engine, "fn": fn, "deps": ds, "dma": dma,
                         "need_inc": dma})
        return len(self.ops) - 1

    def finalize(self):
        for op in self.ops:
            for d in op["deps"]:
                dop = self.ops[d]
                if dop["e"] != op["e"] or dop["dma"]:
                    dop["need_inc"] = True
        cnt = {}
        self.n_dma = 0
        for op in self.ops:
            if op["dma"]:
                op["dma_idx"] = self.n_dma
                self.n_dma += 1
                op["inc_val"] = 16
            elif op["need_inc"]:
                cnt[op["e"]] = cnt.get(op["e"], 0) + 1
                op["inc_val"] = cnt[op["e"]]
            else:
                op["inc_val"] = None

    def replay(self, engine, handle, sems, dma_sems):
        hw = {}
        dma_waited = set()
        drained_at = -1
        pos = {}
        n_here = 0
        for i, op in enumerate(self.ops):
            if op["e"] != engine:
                continue
            need_drain = False
            for d in op["deps"]:
                dop = self.ops[d]
                if dop["dma"]:
                    di = dop["dma_idx"]
                    if di not in dma_waited:
                        handle.wait_ge(dma_sems[di], 16)
                        dma_waited.add(di)
                    continue
                if dop["e"] == engine:
                    if engine in ("vector", "scalar") and d in pos \
                            and pos[d] >= n_here - 3 and pos[d] > drained_at:
                        need_drain = True
                    continue
                v = dop["inc_val"]
                if v is None:
                    continue
                if hw.get(dop["e"], 0) < v:
                    handle.wait_ge(sems[dop["e"]], v)
                    hw[dop["e"]] = v
            if need_drain:
                handle.drain()
                drained_at = n_here - 1
            instr = op["fn"](handle)
            if op["dma"]:
                instr.then_inc(dma_sems[op["dma_idx"]], 16)
            elif op["need_inc"]:
                instr.then_inc(sems[engine], 1)
            pos[i] = n_here
            n_here += 1


PHASE_MARKS = {}
BUFS = {}
DUMP_AT_LIMIT = None


def build_nc(limit=None, debug_dump=False):
    nc = bass.Bass("TRN2", target_bir_lowering=False,
                   detect_race_conditions=False)
    xs = nc.dram_tensor("xs", [NTOK, D], F32, kind="ExternalInput").ap()
    xt = nc.dram_tensor("xt", [TAILC, D], F32, kind="ExternalInput").ap()
    wf = nc.dram_tensor("wf", [D, D], F32, kind="ExternalInput").ap()
    wv = nc.dram_tensor("wv", [D, D], F32, kind="ExternalInput").ap()
    wo = nc.dram_tensor("wo", [D, D], F32, kind="ExternalInput").ap()
    wk = nc.dram_tensor("wk", [64, D], F32, kind="ExternalInput").ap()
    wq = nc.dram_tensor("wq", [64, D], F32, kind="ExternalInput").ap()
    nwr = nc.dram_tensor("nwr", [1, D], F32, kind="ExternalInput").ap()
    biasr = nc.dram_tensor("biasr", [1, D], F32, kind="ExternalInput").ap()
    alphar = nc.dram_tensor("alphar", [128, 1], F32, kind="ExternalInput").ap()
    etar = nc.dram_tensor("etar", [1, 64], F32, kind="ExternalInput").ap()
    idenr = nc.dram_tensor("idenr", [128, 128], F32, kind="ExternalInput").ap()
    um01r = nc.dram_tensor("um01r", [64, 64], F32, kind="ExternalInput").ap()
    umbr = nc.dram_tensor("umbr", [64, 64], F32, kind="ExternalInput").ap()
    ys = nc.dram_tensor("ys", [NTOK, D], F32, kind="ExternalOutput").ap()

    BUFS.clear()

    def sb(name, shape, dt=F32):
        ap = nc.alloc_sbuf_tensor(name, shape, dt).ap()
        BUFS[name] = ap
        return ap

    XRES = sb("XRES", [128, TT * D])
    LNBUF = sb("LNBUF", [128, D])
    SQS = sb("SQS", [128, D])
    LNH = sb("LNH", [128, D], BF16)
    LNL = sb("LNL", [128, D], BF16)
    LNB2H = sb("LNB2H", [128, D], BF16)
    LNB2L = sb("LNB2L", [128, D], BF16)
    XTH = sb("XTH", [128, D], BF16)
    XTL = sb("XTL", [128, D], BF16)
    XTH2 = sb("XTH2", [128, D], BF16)
    XTL2 = sb("XTL2", [128, D], BF16)
    XT = sb("XT", [128, D])
    WT1 = sb("WT1", [128, DC * D], BF16)
    WXTH = sb("WXTH", [128, DC * D], BF16)
    WXTL = sb("WXTL", [128, DC * D], BF16)
    WSTAGE = sb("WSTAGE", [128, D])
    XT2 = WSTAGE
    WKTH = sb("WKTH", [128, 512], BF16)
    WKTL = sb("WKTL", [128, 512], BF16)
    WQT = sb("WQT", [128, 512])
    NWRP = sb("NWRP", [128, D])
    IDEN = sb("IDEN", [128, 128])
    IDENB = sb("IDENB", [128, 128], BF16)
    UM01 = sb("UM01", [64, 64])
    UMB = sb("UMB", [64, 64])
    ONE1 = sb("ONE1", [1, 128])
    ONES2 = sb("ONES2", [128, 128])
    HBr = sb("HBr", [1, D])
    BR = sb("BR", [1, D])
    NWrow = sb("NWrow", [1, D])
    AL = sb("AL", [128, 1])
    st = {}
    for nm in ("s1", "q1", "m1", "r1", "nb1",
               "s2", "q2", "m2", "r2", "nb2", "rrs", "opr", "mr2",
               "s3", "q3", "m3", "r3", "nb3", "tA", "tB", "tC"):
        st[nm] = sb("st_" + nm, [128, 16])
    ASUM = sb("ASUM", [128, 8])
    DLT = sb("DLT", [128, 1])
    NDLT = sb("NDLT", [128, 1])
    HA = sb("HA", [128, 1])
    TOT = sb("TOT", [1, 128])
    U8 = sb("U8", [64, NTOK])
    XTs = U8[:, 0:D]
    X1T = U8[:, D:2 * D]
    QT = U8[:, 0:NTOK]
    TMPT = sb("TMPT", [64, D])
    TMH = sb("TMH", [64, D], BF16)
    TML = sb("TML", [64, D], BF16)
    XTTH = sb("XTTH", [128, 512], BF16)
    XTTL = sb("XTTL", [128, 512], BF16)
    VSB = sb("VSB", [64, D])
    VTT = sb("VTT", [128, 512])
    MSB = sb("MSB", [64, D])
    TQ1 = sb("TQ1", [64, 512])
    TQ2 = sb("TQ2", [64, 512])
    m6 = {}
    for nm in ("ksb", "ke", "kT", "keT", "G", "G2", "H", "GU", "GHU", "G2HU",
               "ET", "DMT", "EG", "EGH", "EG2H", "ATm", "AR", "TM1", "ETARP"):
        m6[nm] = sb("m_" + nm, [64, 64])
    col_names = ("sig", "kap", "kts", "ubar", "ksg", "Hd", "G2d", "gdh",
                 "cc", "lamc", "phic", "cu", "mk", "ph", "th", "t2", "t3",
                 "t4", "q1c", "u1", "mk2", "qh", "t5", "c2", "t6", "t7", "t8",
                 "t9", "s2c", "t10", "t11", "nlc", "e1", "psi", "muc", "nms",
                 "c1t", "v1t", "m1t", "r1t", "nb1t", "c2t", "v2t", "m2t",
                 "r2t", "rst", "sb2t", "vpe", "vf", "tz1", "tz2")
    colbuf = sb("COLS", [64, 64])
    col = {nm: colbuf[:, i:i + 1] for i, nm in enumerate(col_names)}
    IDX_S2C = col_names.index("s2c")
    IDX_MUC = col_names.index("muc")
    MV = sb("MV", [64, 4])
    LNR = sb("LNR", [1, 65])
    LAMR = sb("LAMR", [1, 64])
    PHR = sb("PHR", [1, 65])
    S2R = sb("S2R", [1, 64])
    RECR = sb("RECR", [1, 64])
    ETAR = sb("ETAR", [1, 64])
    LAMC1 = sb("LAMC1", [1, 1])
    MS1 = sb("MS1", [1, 1])
    WKS = X1T
    WQS = TMPT

    def psalloc(name, shape):
        return nc.alloc_psum_tensor(name, shape, F32).ap()

    psT = psalloc("psT", [128, 512])
    psT2 = psalloc("psT2", [128, 512])
    psA = psalloc("psA", [128, 512])
    psB = psalloc("psB", [128, 512])
    psC = psalloc("psC", [128, 512])
    psD = psalloc("psD", [128, 512])
    psS1 = psalloc("psS1", [64, 512])
    psS2 = psalloc("psS2", [64, 512])
    BANKS = {"psT": psT, "psT2": psT2, "psA": psA, "psB": psB,
             "psC": psC, "psD": psD, "psS1": psS1, "psS2": psS2}

    P = Prog()
    ps_writer = {b: None for b in BANKS}
    ps_reader = {b: [] for b in BANKS}

    def tw(bank, fn, deps=()):
        d = list(deps) + [oid for (oid, _) in ps_reader[bank]]
        i = P.add("tensor", fn, d)
        ps_writer[bank] = i
        ps_reader[bank] = []
        return i

    def prd(engine, bank, fn, deps=()):
        d = (list(deps) + [ps_writer[bank]]
             + [oid for (oid, eng2) in ps_reader[bank] if eng2 != engine])
        i = P.add(engine, fn, d)
        ps_reader[bank].append((i, engine))
        return i

    def V(fn, deps=()):
        return P.add("vector", fn, deps)

    def S(fn, deps=()):
        return P.add("scalar", fn, deps)

    def LD(fn, deps=()):
        return P.add("sync", fn, deps, dma=True)

    def STD(fn, deps=()):
        return P.add("gpsimd", fn, deps, dma=True)

    d_iden = LD(lambda e: e.dma_start(out=IDEN[:], in_=idenr))
    o_idenb = V(lambda e: e.tensor_copy(IDENB[:], IDEN[:]), [d_iden])
    d_um01 = LD(lambda e: e.dma_start(out=UM01[:], in_=um01r))
    d_umb = LD(lambda e: e.dma_start(out=UMB[:], in_=umbr))
    d_nw = LD(lambda e: e.dma_start(out=NWrow[:], in_=nwr))
    d_bias = LD(lambda e: e.dma_start(out=BR[:], in_=biasr))
    d_al = LD(lambda e: e.dma_start(out=AL[:], in_=alphar))
    d_eta = LD(lambda e: e.dma_start(out=ETAR[:], in_=etar))
    d_xt = LD(lambda e: e.dma_start(out=XTs[:], in_=xt))

    o_ones1 = V(lambda e: e.memset(ONE1[:], 1.0))
    o_ones2 = V(lambda e: e.memset(ONES2[:], 1.0))
    o_z1 = V(lambda e: e.memset(LNR[:, 0:1], 0.0))
    o_z2 = V(lambda e: e.memset(PHR[:, 0:1], 0.0))
    o_z3 = V(lambda e: e.memset(colbuf[:], 0.0))
    o_zlam = V(lambda e: e.memset(LAMR[:], 0.0))

    o_hb = V(lambda e: e.tensor_scalar_mul(HBr[:], BR[:], 0.5), [d_bias])
    o_ha2 = V(lambda e: e.tensor_scalar_mul(HA[:], AL[:], 0.5), [d_al])
    o_nw1 = tw("psA", lambda e: e.matmul(psA[:, 0:512], ONE1[:],
                                         NWrow[0:1, 0:512],
                                         start=True, stop=True),
               [o_ones1, d_nw])
    o_nw1e = prd("vector", "psA", lambda e: e.tensor_copy(NWRP[:, 0:512],
                                                          psA[:, 0:512]))
    o_nw2 = tw("psB", lambda e: e.matmul(psB[:, 0:512], ONE1[:],
                                         NWrow[0:1, 512:1024],
                                         start=True, stop=True),
               [o_ones1, d_nw])
    o_nw2e = prd("vector", "psB", lambda e: e.tensor_copy(NWRP[:, 512:1024],
                                                          psB[:, 0:512]))

    def prep_weight(src_hbm, dests, scale_nw, first_dma_deps=(), split=False):
        all_evacs = []
        prev_stage_readers = list(first_dma_deps)
        sums = []
        dbuf = split != "tern"
        prev2 = {0: list(prev_stage_readers), 1: list(prev_stage_readers)}
        for r in range(8):
            stg = (XT if (dbuf and r % 2 == 1) else WSTAGE)
            dma = LD(lambda e, r=r, stg=stg: e.dma_start(
                out=stg[:],
                in_=src_hbm[r * 128:(r + 1) * 128, :]),
                prev2[r % 2 if dbuf else 0])
            base = [dma]
            if scale_nw:
                base.append(V(lambda e, stg=stg: e.tensor_tensor(
                    stg[:], stg[:], NWRP[:], ALU.mult),
                    [dma, o_nw1e, o_nw2e]))
            if src_hbm is wf:
                sums.append(S(lambda e, r=r, stg=stg: e.activation(
                    SQS[:], stg[:], AF.Abs,
                    accum_out=ASUM[:, r:r + 1]), base))
            prev_stage_readers = list(sums[-1:])
            for grp in range(2):
                bank = "psT" if grp == 0 else "psT2"
                pst = BANKS[bank]
                tids = []
                for g in range(4):
                    dc = grp * 4 + g
                    tids.append(tw(bank, lambda e, dc=dc, g=g, pst=pst,
                                   stg=stg: e.transpose(
                        pst[:, g * 128:(g + 1) * 128],
                        stg[:, dc * 128:(dc + 1) * 128],
                        IDEN[:]), base + [d_iden]))
                if split == "tern":
                    for g in range(4):
                        dc = grp * 4 + g
                        cs = slice(dc * D + r * 128, dc * D + (r + 1) * 128)
                        e1 = prd("vector", bank, lambda e, g=g, pst=pst:
                                 e.tensor_single_scalar(
                            LNBUF[:, 0:128], pst[:, g * 128:(g + 1) * 128],
                            DLT[:], ALU.is_gt), [o_dlbe])
                        e2 = prd("vector", bank, lambda e, g=g, pst=pst:
                                 e.tensor_single_scalar(
                            LNBUF[:, 128:256], pst[:, g * 128:(g + 1) * 128],
                            NDLT[:], ALU.is_lt), [o_ndl, e1])
                        e3 = V(lambda e: e.tensor_sub(
                            LNBUF[:, 0:128], LNBUF[:, 0:128],
                            LNBUF[:, 128:256]), [e2])
                        e4 = V(lambda e, cs=cs: e.tensor_scalar_mul(
                            dests[:, cs], LNBUF[:, 0:128], HA[:]),
                            [e3, o_ha2])
                        all_evacs.append(e4)
                elif split:
                    hi, lo = dests
                    for g in range(4):
                        dc = grp * 4 + g
                        cs = slice(dc * D + r * 128, dc * D + (r + 1) * 128)
                        e1 = prd("vector", bank, lambda e, cs=cs, g=g,
                                 pst=pst, hi=hi: e.tensor_copy(
                            hi[:, cs], pst[:, g * 128:(g + 1) * 128]))
                        e2 = prd("scalar", bank, lambda e, g=g, pst=pst:
                                 e.activation(SQS[:, 0:128],
                                              pst[:, g * 128:(g + 1) * 128],
                                              AF.Identity), [e1])
                        e3 = V(lambda e, cs=cs, hi=hi, lo=lo: e.tensor_sub(
                            lo[:, cs], SQS[:, 0:128], hi[:, cs]), [e2, e1])
                        all_evacs += [e1, e3]
                else:
                    dest = dests
                    for g in range(4):
                        dc = grp * 4 + g
                        cs = slice(dc * D + r * 128, dc * D + (r + 1) * 128)
                        all_evacs.append(prd(
                            "vector", bank, lambda e, cs=cs, g=g, pst=pst,
                            dest=dest: e.tensor_copy(
                                dest[:, cs], pst[:, g * 128:(g + 1) * 128])))
                prev_stage_readers += tids
                prev2[r % 2 if dbuf else 0] = list(tids)
        return all_evacs, sums

    wf_sums = []
    prev_r = {0: [], 1: []}
    for r in range(8):
        buf = XT if r % 2 == 0 else LNBUF
        dma = LD(lambda e, r=r, buf=buf: e.dma_start(
            out=buf[:], in_=wf[r * 128:(r + 1) * 128, :]), prev_r[r % 2])
        wf_sums.append(S(lambda e, r=r, buf=buf: e.activation(
            SQS[:], buf[:], AF.Abs, accum_out=ASUM[:, r:r + 1]), [dma]))
        prev_r[r % 2] = [wf_sums[-1]]

    o_as1 = V(lambda e: e.tensor_add(ASUM[:, 0:4], ASUM[:, 0:4],
                                     ASUM[:, 4:8]), wf_sums)
    o_as2 = V(lambda e: e.tensor_add(ASUM[:, 0:2], ASUM[:, 0:2],
                                     ASUM[:, 2:4]), [o_as1])
    o_as7 = V(lambda e: e.tensor_add(ASUM[:, 0:1], ASUM[:, 0:1],
                                     ASUM[:, 1:2]), [o_as2])
    o_tot = tw("psS1", lambda e: e.matmul(psS1[0:8, 0:128], ASUM[:, 0:8],
                                          ONES2[:], start=True, stop=True),
               [o_as7, o_ones2])
    o_tote = prd("vector", "psS1", lambda e: e.tensor_copy(TOT[:],
                                                           psS1[0:1, 0:128]))
    o_dl = V(lambda e: e.tensor_scalar_mul(TOT[:], TOT[:],
                                           0.7 / (1024.0 * 1024.0)), [o_tote])
    o_dlb = tw("psA", lambda e: e.matmul(psA[:, 0:128], ONE1[:], TOT[:],
                                         start=True, stop=True),
               [o_dl, o_ones1])
    o_dlbe = prd("vector", "psA", lambda e: e.tensor_copy(DLT[:],
                                                          psA[:, 0:1]))
    o_ndl = V(lambda e: e.tensor_scalar_mul(NDLT[:], DLT[:], -1.0), [o_dlbe])

    tern_ops, _ = prep_weight(wf, WT1, scale_nw=False,
                              first_dma_deps=[], split="tern")

    d_x = [STD(lambda e, j=j: e.dma_start(
        out=XRES[:, j * D:(j + 1) * D], in_=xs[j * 128:(j + 1) * 128, :]))
        for j in range(TT)]

    def batch_stats(s_t, q_t, m_t, r_t, nb_t, deps, extra=None):
        b1 = V(lambda e: e.tensor_scalar_mul(m_t[:], s_t[:], 1.0 / D), deps)
        b2 = V(lambda e: e.tensor_scalar_mul(st["tA"][:], q_t[:], 1.0 / D),
               deps)
        b3 = V(lambda e: e.tensor_tensor(st["tB"][:], m_t[:], m_t[:],
                                         ALU.mult), [b1])
        b4 = V(lambda e: e.tensor_sub(st["tA"][:], st["tA"][:], st["tB"][:]),
               [b2, b3])
        b5 = V(lambda e: e.tensor_single_scalar(st["tB"][:], st["tA"][:],
                                                EPS_LN, ALU.add), [b4])
        b6 = S(lambda e: e.activation(r_t[:], st["tB"][:], AF.Sqrt), [b5])
        b7 = V(lambda e: e.reciprocal(r_t[:], r_t[:]), [b6])
        b8 = V(lambda e: e.tensor_tensor(nb_t[:], m_t[:], r_t[:], ALU.mult),
               [b7])
        b9 = V(lambda e: e.tensor_scalar_mul(nb_t[:], nb_t[:], -1.0), [b8])
        out = [b9, b7]
        if extra == "rms2":
            c1 = V(lambda e: e.reciprocal(st["tC"][:], st["tB"][:]), [b5, b7])
            c2 = V(lambda e: e.tensor_tensor(st["tC"][:], st["tA"][:],
                                             st["tC"][:], ALU.mult), [c1])
            c3 = V(lambda e: e.tensor_single_scalar(st["tC"][:], st["tC"][:],
                                                    EPS_RMS, ALU.add), [c2])
            c4 = S(lambda e: e.activation(st["tC"][:], st["tC"][:], AF.Sqrt),
                   [c3])
            c5 = V(lambda e: e.reciprocal(st["tC"][:], st["tC"][:]), [c4])
            c6 = V(lambda e: e.tensor_tensor(st["rrs"][:], r_t[:],
                                             st["tC"][:], ALU.mult), [c5])
            c7 = V(lambda e: e.tensor_single_scalar(st["opr"][:], r_t[:], 1.0,
                                                    ALU.add), [b7])
            c8 = V(lambda e: e.tensor_tensor(st["mr2"][:], m_t[:], r_t[:],
                                             ALU.mult), [b7, b9])
            c9 = V(lambda e: e.tensor_tensor(nb_t[:], m_t[:], st["rrs"][:],
                                             ALU.mult), [c6, b9])
            c10 = V(lambda e: e.tensor_scalar_mul(nb_t[:], nb_t[:], -1.0),
                    [c9])
            out = [c6, c7, c8, c10]
        return out

    _FFN1_EMITTED = set()
    stats1 = []
    sum_ops2 = []
    ffn1_done = {}
    _state = {"prev": tern_ops[-1:], "emit_i": 0}

    wv_evacs, _ = prep_weight(wv, (WXTH, WXTL), scale_nw=True,
                              first_dma_deps=tern_ops[-8:], split=True)

    sum_ops1 = []
    for j in range(TT):
        sum_ops1.append(V(lambda e, j=j: e.reduce_sum(
            st["s1"][:, j:j + 1], XRES[:, j * D:(j + 1) * D], axis=AX.X),
            [d_x[j]]))
        sum_ops1.append(S(lambda e, j=j: e.activation(
            LNB2L[:], XRES[:, j * D:(j + 1) * D], AF.Square,
            accum_out=st["q1"][:, j:j + 1]), [d_x[j]]))
    stats1.extend(batch_stats(st["s1"], st["q1"], st["m1"], st["r1"],
                              st["nb1"], sum_ops1))


    d_wk = LD(lambda e: e.dma_start(out=WKS[:], in_=wk))
    d_wq = LD(lambda e: e.dma_start(out=WQS[:], in_=wq))
    o_wks = V(lambda e: e.tensor_tensor(WKS[:], WKS[:], NWRP[0:64, :],
                                        ALU.mult), [d_wk, o_nw1e, o_nw2e])
    o_wqs = V(lambda e: e.tensor_tensor(WQS[:], WQS[:], NWRP[0:64, :],
                                        ALU.mult), [d_wq, o_nw1e, o_nw2e])
    for dc in range(DC):
        tw("psT", lambda e, dc=dc: e.transpose(
            psT[:, dc * 64:(dc + 1) * 64],
            WKS[0:64, dc * 128:(dc + 1) * 128], IDEN[0:64, 0:64]),
            [o_wks, d_iden] if dc == 0 else [])
    o_wkh = prd("vector", "psT", lambda e: e.tensor_copy(WKTH[:], psT[:]))
    o_wkf = prd("scalar", "psT", lambda e: e.activation(SQS[:, 0:512],
                                                        psT[:], AF.Identity),
                [o_wkh])
    o_wkt = V(lambda e: e.tensor_sub(WKTL[:], SQS[:, 0:512], WKTH[:]),
              [o_wkf, o_wkh])
    for dc in range(DC):
        tw("psT2", lambda e, dc=dc: e.transpose(
            psT2[:, dc * 64:(dc + 1) * 64],
            WQS[0:64, dc * 128:(dc + 1) * 128], IDEN[0:64, 0:64]),
            [o_wqs, d_iden] if dc == 0 else [])
    o_wqt = prd("vector", "psT2", lambda e: e.tensor_copy(WQT[:], psT2[:]))

    o_et1 = S(lambda e: e.activation(ETAR[:], ETAR[:], AF.Abs), [d_eta])
    o_et2 = V(lambda e: e.tensor_single_scalar(ETAR[:], ETAR[:], ETA_MIN,
                                               ALU.max), [o_et1])
    o_etp = tw("psS2", lambda e: e.matmul(psS2[0:64, 0:64], ONE1[0:1, 0:64],
                                          ETAR[:], start=True, stop=True),
               [o_et2, o_ones1])
    o_etpe = prd("vector", "psS2", lambda e: e.tensor_copy(m6["ETARP"][:],
                                                           psS2[0:64, 0:64]))


    def emit_ffn_tile(j, r_t, nb_t, src_res, sums, sumcols, f1,
                      force_par=None):
        if force_par is None:
            par = _state["emit_i"] % 2
            _state["emit_i"] += 1
        else:
            par = force_par
            _state["emit_i"] = par + 1
        xh = XTH if par == 0 else XTH2
        xl = XTL if par == 0 else XTL2
        bk = (("psA", "psB") if par == 0 else ("psC", "psD"))
        o_h = S(lambda e, j=j: e.activation(
            LNH[:], XRES[:, j * D:(j + 1) * D], AF.Identity,
            scale=r_t[:, j:j + 1], bias=nb_t[:, j:j + 1]),
            src_res + _state["prev"])
        o_f = V(lambda e, j=j: e.tensor_scalar(
            LNBUF[:], XRES[:, j * D:(j + 1) * D], r_t[:, j:j + 1],
            nb_t[:, j:j + 1], ALU.mult, ALU.add), src_res + _state["prev"])
        o_l = V(lambda e: e.tensor_sub(LNL[:, 0:512], LNBUF[:, 0:512],
                                       LNH[:, 0:512]), [o_f, o_h])
        o_l2 = V(lambda e: e.tensor_sub(LNL[:, 512:1024], LNBUF[:, 512:1024],
                                        LNH[:, 512:1024]), [o_f, o_h])
        t_ops = []
        for bank, srcb, dstb, dep, evac_eng in (
                ("psT", LNH, xh, [o_h], "vector"),
                ("psT2", LNL, xl, [o_l, o_l2], "scalar")):
            pst = BANKS[bank].bitcast(BF16)
            for dc in range(8):
                tw(bank, lambda e, dc=dc, pst=pst, srcb=srcb: e.transpose(
                    pst[:, dc * 128:(dc + 1) * 128],
                    srcb[:, dc * 128:(dc + 1) * 128], IDENB[:]),
                    (dep + [o_idenb]) if dc == 0 else [])
            if evac_eng == "vector":
                t_ops.append(prd("vector", bank,
                                 lambda e, pst=pst, dstb=dstb: e.tensor_copy(
                    dstb[:], pst[:])))
            else:
                t_ops.append(prd("scalar", bank,
                                 lambda e, pst=pst, dstb=dstb: e.activation(
                    dstb[:], pst[:], AF.Identity)))
        mm_last = []
        for oh in (0, 1):
            bank = bk[oh]
            pbank = BANKS[bank]
            for dc in range(DC):
                for xb in (xh, xl):
                    tw(bank, lambda e, dc=dc, oh=oh, pbank=pbank, xb=xb:
                       e.matmul(
                        pbank[:, 0:512], xb[:, dc * 128:(dc + 1) * 128],
                        WT1[:, dc * D + oh * 512:dc * D + oh * 512 + 512],
                        start=(dc == 0 and xb is xh), stop=False),
                        (t_ops + tern_ops[-1:])
                        if dc == 0 and xb is xh else [])
            mm_last.append(tw(bank, lambda e, oh=oh, pbank=pbank: e.matmul(
                pbank[:, 0:512], ONE1[:], HBr[0:1, oh * 512:oh * 512 + 512],
                start=False, stop=True), [o_hb, o_ones1]))
        _state["prev"] = t_ops[-2:]

        def finish():
            res_ops = []
            for oh in (0, 1):
                bank = bk[oh]
                pbank = BANKS[bank]
                res_ops.append(prd("vector", bank,
                                   lambda e, j=j, oh=oh, pbank=pbank:
                                   e.tensor_tensor(
                    XRES[:, j * D + oh * 512:j * D + oh * 512 + 512],
                    XRES[:, j * D + oh * 512:j * D + oh * 512 + 512],
                    pbank[:, 0:512], ALU.add), src_res))
            if sums is not None:
                sums.append(V(lambda e, j=j, sumcols=sumcols: e.reduce_sum(
                    sumcols[0][:, j:j + 1], XRES[:, j * D:(j + 1) * D],
                    axis=AX.X), res_ops))
                sums.append(S(lambda e, j=j, sumcols=sumcols: e.activation(
                    SQS[:], XRES[:, j * D:(j + 1) * D], AF.Square,
                    accum_out=sumcols[1][:, j:j + 1]), res_ops))
            return res_ops
        return finish

    _fin_q = []

    def flush_fin():
        while _fin_q:
            j, fin = _fin_q.pop(0)
            ffn1_done[j] = fin()

    def emit_ffn1_tile(j, force_par=None):
        fin = emit_ffn_tile(j, st["r1"], st["nb1"],
                            stats1 + [d_x[j]], sum_ops2,
                            (st["s2"], st["q2"]), True, force_par=force_par)
        _fin_q.append((j, fin))

    PHASE_MARKS["p0_done"] = len(P.ops)
    tail_start = [d_xt, o_wkt, o_wqt, o_wkh]

    def ln_stats_small(x_ap, sum_c, var_c, m_c, r_c, nb_c, deps, rms_rs=None):
        o1 = V(lambda e: e.reduce_sum(sum_c, x_ap, axis=AX.X), deps)
        o2 = S(lambda e: e.activation(TMPT[:], x_ap, AF.Square,
                                      accum_out=var_c), deps)
        o3 = V(lambda e: e.tensor_scalar_mul(m_c, sum_c, 1.0 / D), [o1])
        o4 = V(lambda e: e.tensor_scalar_mul(var_c, var_c, 1.0 / D), [o2])
        o5 = V(lambda e: e.tensor_tensor(nb_c, m_c, m_c, ALU.mult), [o3])
        o6 = V(lambda e: e.tensor_sub(var_c, var_c, nb_c), [o4, o5])
        o7 = V(lambda e: e.tensor_single_scalar(nb_c, var_c, EPS_LN, ALU.add),
               [o6])
        o8 = S(lambda e: e.activation(r_c, nb_c, AF.Sqrt), [o7])
        o9 = V(lambda e: e.reciprocal(r_c, r_c), [o8])
        ops = [o9]
        if rms_rs is not None:
            rs_c, vpe_tmp, vf_tmp = rms_rs
            r1_ = V(lambda e: e.reciprocal(vpe_tmp, nb_c), [o7, o9])
            r2_ = V(lambda e: e.tensor_tensor(vf_tmp, var_c, vpe_tmp,
                                              ALU.mult), [r1_])
            r3_ = V(lambda e: e.tensor_single_scalar(vf_tmp, vf_tmp, EPS_RMS,
                                                     ALU.add), [r2_])
            r4_ = S(lambda e: e.activation(rs_c, vf_tmp, AF.Sqrt), [r3_])
            r5_ = V(lambda e: e.reciprocal(rs_c, rs_c), [r4_])
            ops.append(r5_)
        o10 = V(lambda e: e.tensor_tensor(nb_c, m_c, r_c, ALU.mult), [o9])
        o11 = V(lambda e: e.tensor_scalar_mul(nb_c, nb_c, -1.0), [o10])
        return ops + [o11]

    def tail_split_transpose(src64, scale_c, bias_c, deps):
        o_h = S(lambda e: e.activation(TMH[:], src64, AF.Identity,
                                       scale=scale_c, bias=bias_c), deps)
        o_f = V(lambda e: e.tensor_scalar(TMPT[:], src64, scale_c, bias_c,
                                          ALU.mult, ALU.add), deps)
        o_l = V(lambda e: e.tensor_sub(TML[:, 0:512], TMPT[:, 0:512],
                                       TMH[:, 0:512]), [o_f, o_h])
        o_l2 = V(lambda e: e.tensor_sub(TML[:, 512:1024], TMPT[:, 512:1024],
                                        TMH[:, 512:1024]), [o_f, o_h])
        outs = []
        for bank, srcb, dstb, dep in (("psT", TMH, XTTH, [o_h]),
                                      ("psT2", TML, XTTL, [o_l, o_l2])):
            pst = BANKS[bank].bitcast(BF16)
            for dc in range(8):
                tw(bank, lambda e, dc=dc, pst=pst, srcb=srcb: e.transpose(
                    pst[:, dc * 64:(dc + 1) * 64],
                    srcb[:, dc * 128:(dc + 1) * 128], IDENB[0:64, 0:64]),
                    (dep + [o_idenb]) if dc == 0 else [])
            outs.append(prd("vector", bank, lambda e, pst=pst, dstb=dstb:
                            e.tensor_copy(dstb[:], pst[:, 0:512])))
        return outs

    s_ln1t = ln_stats_small(XTs[:], col["c1t"], col["v1t"], col["m1t"],
                            col["r1t"], col["nb1t"], tail_start)
    t_lnT = tail_split_transpose(XTs[:], col["r1t"], col["nb1t"],
                                 s_ln1t + tail_start)
    flush_fin()
    emit_ffn1_tile(0, force_par=1)
    _FFN1_EMITTED.add(0)

    x1t_done = []
    for oh, bank in ((0, "psA"), (1, "psB")):
        pbank = BANKS[bank]
        for dc in range(DC):
            for xb in (XTTH, XTTL):
                tw(bank, lambda e, dc=dc, oh=oh, pbank=pbank, xb=xb: e.matmul(
                    pbank[0:64, 0:512], xb[:, dc * 64:(dc + 1) * 64],
                    WT1[:, dc * D + oh * 512:dc * D + oh * 512 + 512],
                    start=(dc == 0 and xb is XTTH), stop=False),
                    (t_lnT + tern_ops) if dc == 0 and xb is XTTH else [])
        tw(bank, lambda e, oh=oh, pbank=pbank: e.matmul(
            pbank[0:64, 0:512], ONE1[0:1, 0:64],
            HBr[0:1, oh * 512:oh * 512 + 512],
            start=False, stop=True), [o_hb, o_ones1])
        x1t_done.append(prd("vector", bank,
                            lambda e, oh=oh, pbank=pbank: e.tensor_tensor(
            X1T[:, oh * 512:oh * 512 + 512], XTs[:, oh * 512:oh * 512 + 512],
            pbank[0:64, 0:512], ALU.add), tail_start))
    flush_fin()
    emit_ffn1_tile(1, force_par=1)
    _FFN1_EMITTED.add(1)

    s_ln2t = ln_stats_small(X1T[:], col["c2t"], col["v2t"], col["m2t"],
                            col["r2t"], col["sb2t"], x1t_done,
                            rms_rs=(col["rst"], col["vpe"], col["vf"]))
    o_sc1 = V(lambda e: e.tensor_tensor(col["tz1"], col["r2t"], col["rst"],
                                        ALU.mult), s_ln2t)
    o_sc2 = V(lambda e: e.tensor_tensor(col["tz2"], col["m2t"], col["tz1"],
                                        ALU.mult), [o_sc1])
    o_sc3 = V(lambda e: e.tensor_scalar_mul(col["tz2"], col["tz2"], -1.0),
              [o_sc2])
    t_xnT = tail_split_transpose(X1T[:], col["tz1"], col["tz2"],
                                 [o_sc3] + t_lnT)
    flush_fin()
    emit_ffn1_tile(2, force_par=1)
    _FFN1_EMITTED.add(2)

    for i3, (xb, wb) in enumerate(((XTTH, WKTH), (XTTH, WKTL),
                                   (XTTL, WKTH))):
        for dc in range(DC):
            tw("psS2", lambda e, dc=dc, xb=xb, wb=wb, i3=i3: e.matmul(
                psS2[0:64, 64:128], xb[:, dc * 64:(dc + 1) * 64],
                wb[:, dc * 64:(dc + 1) * 64],
                start=(dc == 0 and i3 == 0),
                stop=(dc == DC - 1 and i3 == 2)),
                (t_xnT + [o_wkt, o_wkh]) if dc == 0 and i3 == 0 else [])
    o_kr = prd("scalar", "psS2", lambda e: e.activation(TQ1[:, 0:64],
                                                        psS2[0:64, 64:128],
                                                        AF.Relu))
    o_km = prd("vector", "psS2", lambda e: e.tensor_single_scalar(
        TQ2[:, 0:64], psS2[0:64, 64:128], 0.0, ALU.min))
    o_kx = S(lambda e: e.activation(TQ2[:, 0:64], TQ2[:, 0:64], AF.Exp),
             [o_km])
    o_k = V(lambda e: e.tensor_tensor(m6["ksb"][:], TQ1[:, 0:64],
                                      TQ2[:, 0:64], ALU.add), [o_kr, o_kx])
    flush_fin()
    emit_ffn1_tile(3, force_par=1)
    _FFN1_EMITTED.add(3)

    v_done = []
    for oh, bank in ((0, "psA"), (1, "psB")):
        pbank = BANKS[bank]
        for i3, (xb, wb) in enumerate(((XTTH, WXTH), (XTTH, WXTL),
                                       (XTTL, WXTH))):
            for dc in range(DC):
                tw(bank, lambda e, dc=dc, oh=oh, pbank=pbank, xb=xb, wb=wb,
                   i3=i3: e.matmul(
                    pbank[0:64, 0:512], xb[:, dc * 64:(dc + 1) * 64],
                    wb[:, dc * D + oh * 512:dc * D + oh * 512 + 512],
                    start=(dc == 0 and i3 == 0),
                    stop=(dc == DC - 1 and i3 == 2)),
                    (t_xnT + wv_evacs) if dc == 0 and i3 == 0 else [])
        v_done.append(prd("vector", bank, lambda e, oh=oh, pbank=pbank:
                          e.tensor_copy(
            VSB[:, oh * 512:oh * 512 + 512], pbank[0:64, 0:512])))
    wo_evacs, _ = prep_weight(wo, (WXTH, WXTL), scale_nw=False,
                              first_dma_deps=v_done, split=True)


    flush_fin()
    emit_ffn1_tile(4, force_par=1)
    _FFN1_EMITTED.add(4)

    o_kee = V(lambda e: e.tensor_tensor(m6["ke"][:], m6["ksb"][:],
                                        m6["ETARP"][:], ALU.mult),
              [o_k, o_etpe])
    o_ktT = tw("psS2", lambda e: e.transpose(psS2[0:64, 128:192],
                                             m6["ksb"][:], IDEN[0:64, 0:64]),
               [o_k, d_iden])
    o_ktTe = prd("vector", "psS2", lambda e: e.tensor_copy(
        m6["kT"][:], psS2[0:64, 128:192]))
    o_keT = tw("psS2", lambda e: e.transpose(psS2[0:64, 192:256],
                                             m6["ke"][:], IDEN[0:64, 0:64]),
               [o_kee, d_iden])
    o_keTe = prd("vector", "psS2", lambda e: e.tensor_copy(
        m6["keT"][:], psS2[0:64, 192:256]))
    o_G = tw("psS1", lambda e: e.matmul(psS1[0:64, 0:64], m6["keT"][:],
                                        m6["kT"][:], start=True, stop=True),
             [o_ktTe, o_keTe])
    o_Ge = prd("vector", "psS1", lambda e: e.tensor_copy(m6["G"][:],
                                                         psS1[0:64, 0:64]))
    o_G2 = tw("psS1", lambda e: e.matmul(psS1[0:64, 64:128], m6["keT"][:],
                                         m6["keT"][:], start=True, stop=True),
              [o_keTe])
    o_G2e = prd("vector", "psS1", lambda e: e.tensor_copy(m6["G2"][:],
                                                          psS1[0:64, 64:128]))
    for dc in range(8):
        tw("psT", lambda e, dc=dc: e.transpose(
            psT[:, dc * 64:(dc + 1) * 64],
            VSB[:, dc * 128:(dc + 1) * 128], IDEN[0:64, 0:64]),
            (v_done + [d_iden]) if dc == 0 else [])
    t_v = [prd("vector", "psT", lambda e: e.tensor_copy(VTT[:], psT[:]))]
    for dc in range(DC):
        tw("psS1", lambda e, dc=dc: e.matmul(
            psS1[0:64, 128:192], VTT[:, dc * 64:(dc + 1) * 64],
            VTT[:, dc * 64:(dc + 1) * 64],
            start=(dc == 0), stop=(dc == DC - 1)), t_v if dc == 0 else [])
    o_He = prd("vector", "psS1", lambda e: e.tensor_copy(m6["H"][:],
                                                         psS1[0:64, 128:192]))
    o_sig = V(lambda e: e.reduce_sum(col["sig"], VSB[:], axis=AX.X), v_done)
    o_kap = V(lambda e: e.reduce_sum(col["kap"], m6["ksb"][:], axis=AX.X),
              [o_k])
    o_kpt = V(lambda e: e.reduce_sum(col["tz1"], m6["ke"][:], axis=AX.X),
              [o_kee, o_sc1])
    o_kts = V(lambda e: e.tensor_tensor(col["kts"], col["tz1"], col["sig"],
                                        ALU.mult), [o_kpt, o_sig])
    o_ub = V(lambda e: e.tensor_scalar_mul(col["ubar"], col["kts"],
                                           1.0 / NF), [o_kts])
    o_ksg = V(lambda e: e.tensor_tensor(col["ksg"], col["kap"], col["sig"],
                                        ALU.mult), [o_kap, o_sig])
    o_hd1 = V(lambda e: e.tensor_tensor(TQ1[:, 0:64], m6["H"][:],
                                        IDEN[0:64, 0:64], ALU.mult),
              [o_He, d_iden, o_k])
    o_hd = V(lambda e: e.reduce_sum(col["Hd"], TQ1[:, 0:64], axis=AX.X),
             [o_hd1])
    o_gd1 = V(lambda e: e.tensor_tensor(TQ2[:, 0:64], m6["G2"][:],
                                        IDEN[0:64, 0:64], ALU.mult),
              [o_G2e, d_iden, o_k])
    o_gd = V(lambda e: e.reduce_sum(col["G2d"], TQ2[:, 0:64], axis=AX.X),
             [o_gd1])
    o_gdh = V(lambda e: e.tensor_tensor(col["gdh"], col["G2d"], col["Hd"],
                                        ALU.mult), [o_gd, o_hd])
    o_gu = V(lambda e: e.tensor_tensor(m6["GU"][:], m6["G"][:], UM01[:],
                                       ALU.mult), [o_Ge, d_um01])
    o_gh = V(lambda e: e.tensor_tensor(m6["GHU"][:], m6["G"][:], m6["H"][:],
                                       ALU.mult), [o_Ge, o_He])
    o_ghu = V(lambda e: e.tensor_tensor(m6["GHU"][:], m6["GHU"][:], UM01[:],
                                        ALU.mult), [o_gh, d_um01])
    o_g2h = V(lambda e: e.tensor_tensor(m6["G2HU"][:], m6["G2"][:],
                                        m6["H"][:], ALU.mult), [o_G2e, o_He])
    o_g2hu = V(lambda e: e.tensor_tensor(m6["G2HU"][:], m6["G2HU"][:],
                                         UM01[:], ALU.mult), [o_g2h, d_um01])

    flush_fin()
    emit_ffn1_tile(5, force_par=1)
    _FFN1_EMITTED.add(5)

    PHASE_MARKS["tail_done"] = len(P.ops)
    scan_pre = [o_ghu, o_g2hu, o_gu, o_ub, o_ksg, o_gdh, o_z1, o_z2, o_z3,
                o_zlam, d_umb, o_He, o_hd, o_etpe]
    prev_sweep = list(scan_pre)
    for sw in range(K_SWEEPS):
        t1 = tw("psS1", lambda e: e.matmul(psS1[0:64, 256:320],
                                           ONE1[0:1, 0:64], LAMR[:],
                                           start=True, stop=True),
                prev_sweep + [o_ones1, o_zlam])
        v1 = prd("vector", "psS1", lambda e: e.tensor_scalar(
            m6["DMT"][:], psS1[0:64, 256:320], -1.0, col["lamc"],
            ALU.mult, ALU.add), prev_sweep)
        v2 = V(lambda e: e.tensor_single_scalar(m6["DMT"][:], m6["DMT"][:],
                                                CLAMP, ALU.min), [v1])
        v3 = V(lambda e: e.tensor_tensor(m6["DMT"][:], m6["DMT"][:], UMB[:],
                                         ALU.add), [v2, d_umb])
        s1 = S(lambda e: e.activation(m6["ET"][:], m6["DMT"][:], AF.Exp),
               [v3])
        v4 = V(lambda e: e.tensor_tensor(m6["EG"][:], m6["ET"][:],
                                         m6["GU"][:], ALU.mult), [s1, o_gu])
        v5 = V(lambda e: e.tensor_tensor(m6["EGH"][:], m6["ET"][:],
                                         m6["GHU"][:], ALU.mult), [s1, o_ghu])
        v6 = V(lambda e: e.tensor_tensor(m6["EG2H"][:], m6["ET"][:],
                                         m6["G2HU"][:], ALU.mult),
               [s1, o_g2hu])
        v7 = V(lambda e: e.tensor_tensor(col["cu"], col["cc"], col["ubar"],
                                         ALU.mult), prev_sweep + [o_ub])
        v8 = V(lambda e: e.tensor_scalar_mul(m6["ATm"][:], m6["EG"][:],
                                             col["cc"]), [v4] + prev_sweep)
        t2 = tw("psS2", lambda e: e.matmul(psS2[0:64, 0:1], m6["ET"][:],
                                           col["cu"], start=True, stop=True),
                [s1, v7])
        t3 = tw("psS2", lambda e: e.matmul(psS2[0:64, 1:2], m6["EGH"][:],
                                           col["cc"], start=True, stop=True),
                [v5] + prev_sweep)
        t4 = tw("psS2", lambda e: e.matmul(psS2[0:64, 2:3], m6["EG2H"][:],
                                           col["cc"], start=True, stop=True),
                [v6] + prev_sweep)
        t5 = tw("psS2", lambda e: e.matmul(psS2[0:64, 3:4], m6["ATm"][:],
                                           col["sig"], start=True, stop=True),
                [v8, o_sig])
        t6 = tw("psS1", lambda e: e.matmul(psS1[0:64, 320:384], m6["ATm"][:],
                                           m6["H"][:], start=True, stop=True),
                [v8, o_He])
        t7 = tw("psS1", lambda e: e.transpose(psS1[0:64, 384:448],
                                              m6["ATm"][:], IDEN[0:64, 0:64]),
                [v8, d_iden])
        flush_fin()
        if sw < TT and sw not in _FFN1_EMITTED:
            emit_ffn1_tile(sw)
            _FFN1_EMITTED.add(sw)
        v9 = prd("vector", "psS2", lambda e: e.tensor_copy(MV[:],
                                                           psS2[0:64, 0:4]))
        v10 = prd("vector", "psS1", lambda e: e.tensor_copy(
            m6["AR"][:], psS1[0:64, 384:448]))
        v11 = prd("vector", "psS1", lambda e: e.tensor_tensor(
            m6["TM1"][:], psS1[0:64, 320:384], m6["AR"][:], ALU.mult), [v10])
        v12 = V(lambda e: e.reduce_sum(col["q1c"], m6["TM1"][:], axis=AX.X),
                [v11])
        v13 = S(lambda e: e.activation(col["mk"], MV[:, 0:1], AF.Identity,
                                       scale=col["kap"]), [v9, o_kap])
        v14 = V(lambda e: e.tensor_tensor(col["t2"], MV[:, 0:1], col["ksg"],
                                          ALU.mult), [v9, o_ksg])
        v15 = V(lambda e: e.tensor_sub(col["ph"], MV[:, 1:2], col["t2"]),
                [v14, v9])
        v16 = S(lambda e: e.activation(col["t3"], MV[:, 0:1], AF.Identity,
                                       scale=col["kts"]), [v9, o_kts])
        v17 = V(lambda e: e.tensor_sub(col["th"], MV[:, 2:3], col["t3"]),
                [v16, v9])
        v18 = S(lambda e: e.activation(col["t4"], MV[:, 3:4], AF.Identity,
                                       scale=col["mk"]), [v13, v9])
        v19 = V(lambda e: e.scalar_tensor_tensor(col["u1"], col["t4"], -2.0,
                                                 col["q1c"], ALU.mult,
                                                 ALU.add), [v18, v12])
        v20 = S(lambda e: e.activation(col["mk2"], col["mk"], AF.Square),
                [v13])
        v21 = V(lambda e: e.scalar_tensor_tensor(col["qh"], col["mk2"],
                                                 float(D), col["u1"],
                                                 ALU.mult, ALU.add),
                [v20, v19])
        v22 = V(lambda e: e.scalar_tensor_tensor(col["t5"], col["ph"], -2.0,
                                                 col["qh"], ALU.mult,
                                                 ALU.add), [v15, v21])
        v23 = V(lambda e: e.tensor_tensor(col["c2"], col["t5"], col["Hd"],
                                          ALU.add), [v22, o_hd])
        v24 = V(lambda e: e.tensor_single_scalar(col["c2"], col["c2"], 0.0,
                                                 ALU.max), [v23])
        s2 = S(lambda e: e.activation(col["cc"], col["c2"], AF.Sqrt), [v24])
        v25 = S(lambda e: e.activation(col["t6"], col["th"], AF.Identity,
                                       scale=col["cc"]), [s2, v17])
        v26 = S(lambda e: e.activation(col["t7"], col["cc"], AF.Square),
                [s2])
        v27 = V(lambda e: e.tensor_tensor(col["t8"], col["t7"], col["gdh"],
                                          ALU.mult), [v26, o_gdh])
        v28 = V(lambda e: e.scalar_tensor_tensor(col["t9"], col["t6"], 2.0,
                                                 col["t8"], ALU.mult,
                                                 ALU.add), [v25, v27])
        v29 = V(lambda e: e.scalar_tensor_tensor(col["s2c"], col["t9"],
                                                 1.0 / NF, col["phic"],
                                                 ALU.mult, ALU.add),
                [v28] + prev_sweep)
        v30 = S(lambda e: e.activation(col["t10"], col["cc"], AF.Identity,
                                       scale=col["ubar"]), [s2, o_ub])
        v31 = S(lambda e: e.activation(col["t11"], col["t10"], AF.Square),
                [v30])
        v32 = V(lambda e: e.tensor_sub(col["s2c"], col["s2c"], col["t11"]),
                [v29, v31])
        v33 = V(lambda e: e.tensor_single_scalar(col["s2c"], col["s2c"],
                                                 EPS_LN, ALU.add), [v32])
        v34 = V(lambda e: e.tensor_single_scalar(col["s2c"], col["s2c"],
                                                 EPS_LN, ALU.max), [v33])
        v34 = V(lambda e: e.tensor_single_scalar(col["s2c"], col["s2c"],
                                                 1e12, ALU.min), [v34])
        t8 = tw("psS2", lambda e: e.transpose(
            psS2[0:16, 64:128], colbuf[:, IDX_S2C:IDX_S2C + 16],
            IDEN[0:64, 0:64]), [v34, d_iden])
        v35 = prd("vector", "psS2", lambda e: e.tensor_copy(
            S2R[:], psS2[0:1, 64:128]))
        s3 = S(lambda e: e.activation(LNR[:, 1:65], S2R[:], AF.Ln),
               [v35, o_z1])
        v36 = V(lambda e: e.tensor_scalar_mul(LNR[:, 1:65], LNR[:, 1:65],
                                              0.5), [s3])
        v37 = V(lambda e: e.tensor_tensor_scan(LAMR[:], ONE1[0:1, 0:64],
                                               LNR[:, 0:64], 0.0, ALU.mult,
                                               ALU.add), [v36, o_ones1])
        v38 = V(lambda e: e.reciprocal(RECR[:], S2R[:]), [v35])
        v39 = V(lambda e: e.tensor_scalar(PHR[:, 1:65], RECR[:], -EPS_LN,
                                          1.0, ALU.mult, ALU.add),
                [v38, o_z2])
        t9 = tw("psS2", lambda e: e.matmul(psS2[0:64, 128:192], LAMR[:],
                                           ONE1[0:1, 0:64],
                                           start=True, stop=True),
                [v37, o_ones1])
        t10 = tw("psS2", lambda e: e.matmul(psS2[0:64, 192:256],
                                            PHR[:, 0:64], ONE1[0:1, 0:64],
                                            start=True, stop=True),
                 [v39, o_ones1])
        v40 = prd("vector", "psS2", lambda e: e.tensor_copy(
            col["lamc"], psS2[0:64, 128:129]))
        v41 = prd("vector", "psS2", lambda e: e.tensor_copy(
            col["phic"], psS2[0:64, 192:193]))
        prev_sweep = [s2, v37, v40, v41, v34, v36]
        flush_fin()
        if sw + K_SWEEPS < TT and (sw + K_SWEEPS) not in _FFN1_EMITTED:
            emit_ffn1_tile(sw + K_SWEEPS)
            _FFN1_EMITTED.add(sw + K_SWEEPS)

    m_done = []

    def emit_final_m():
        PHASE_MARKS["sweeps_done"] = len(P.ops)
        o_lc1 = V(lambda e: e.tensor_tensor(LAMC1[:], LAMR[:, 63:64],
                                            LNR[:, 64:65], ALU.add), prev_sweep)
        o_lcr = V(lambda e: e.tensor_scalar_mul(RECR[:], ONE1[0:1, 0:64],
                                                LAMC1[:]), [o_lc1, o_ones1])
        o_lcb = tw("psS2", lambda e: e.matmul(psS2[0:64, 0:64], RECR[:],
                                              ONE1[0:1, 0:64],
                                              start=True, stop=True),
                   [o_lcr, o_ones1])
        o_lcbe = prd("vector", "psS2", lambda e: e.tensor_copy(
            col["nlc"], psS2[0:64, 0:1]))
        o_nlc = V(lambda e: e.tensor_scalar_mul(col["nlc"], col["nlc"], -1.0),
                  [o_lcbe])
        o_e1 = S(lambda e: e.activation(col["e1"], col["lamc"], AF.Exp,
                                        bias=col["nlc"]), [o_nlc] + prev_sweep)
        o_psi = V(lambda e: e.tensor_tensor(col["psi"], col["e1"], col["cc"],
                                            ALU.mult), [o_e1] + prev_sweep)
        o_kps = V(lambda e: e.tensor_scalar_mul(TQ1[:, 0:64], m6["ke"][:],
                                                col["psi"]),
                  [o_psi, o_kee, o_hd1])
        mhalf = []
        for oh, bank in ((0, "psA"), (1, "psB")):
            pbank = BANKS[bank]
            mhalf.append(tw(bank, lambda e, oh=oh, pbank=pbank: e.matmul(
                pbank[0:64, 0:512], TQ1[:, 0:64],
                VSB[:, oh * 512:oh * 512 + 512],
                start=True, stop=True), [o_kps] + v_done))
        o_mu = V(lambda e: e.tensor_tensor(col["muc"], col["psi"], col["ubar"],
                                           ALU.mult), [o_psi, o_ub])
        o_ms = tw("psS2", lambda e: e.transpose(
            psS2[0:16, 256:320], colbuf[:, IDX_MUC:IDX_MUC + 16],
            IDEN[0:64, 0:64]), [o_mu, d_iden])
        o_mse = prd("vector", "psS2", lambda e: e.tensor_copy(
            S2R[:], psS2[0:1, 256:320]))
        o_ms2 = V(lambda e: e.reduce_sum(MS1[:], S2R[:], axis=AX.X), [o_mse])
        o_msr = V(lambda e: e.tensor_scalar_mul(RECR[:], ONE1[0:1, 0:64],
                                                MS1[:]), [o_ms2, o_ones1])
        o_msb = tw("psS2", lambda e: e.matmul(psS2[0:64, 320:384], RECR[:],
                                              ONE1[0:1, 0:64],
                                              start=True, stop=True),
                   [o_msr, o_ones1])
        o_msbe = prd("vector", "psS2", lambda e: e.tensor_copy(
            col["nms"], psS2[0:64, 320:321]))
        o_nms = V(lambda e: e.tensor_scalar_mul(col["nms"], col["nms"], -1.0),
                  [o_msbe])
        for oh in (0, 1):
            bank = "psA" if oh == 0 else "psB"
            pbank = BANKS[bank]
            m_done.append(prd("scalar", bank, lambda e, oh=oh, pbank=pbank:
                              e.activation(
                MSB[:, oh * 512:oh * 512 + 512], pbank[0:64, 0:512], AF.Identity,
                bias=col["nms"]), [o_nms, mhalf[oh]]))


    PHASE_MARKS["m_done_mark"] = len(P.ops)
    for j in range(TT):
        if j not in _FFN1_EMITTED:
            emit_ffn1_tile(j)
            _FFN1_EMITTED.add(j)
            flush_fin()
    flush_fin()

    stats2 = batch_stats(st["s2"], st["q2"], st["m2"], st["r2"], st["nb2"],
                         sum_ops2, extra="rms2")

    PHASE_MARKS["ffn1_done_mark"] = len(P.ops)
    prev_tops = _state["prev"]
    phi_done = []
    for j in range(TT):
        if j == 6:
            emit_final_m()
        xtb = XT if j % 2 == 0 else XT2
        o_xn = S(lambda e, j=j: e.activation(
            LNBUF[:], XRES[:, j * D:(j + 1) * D], AF.Identity,
            scale=st["rrs"][:, j:j + 1], bias=st["nb2"][:, j:j + 1]),
            stats2 + ffn1_done[j] + prev_tops)
        t_ops = []
        for grp in range(2):
            bank = "psT" if grp == 0 else "psT2"
            pst = BANKS[bank]
            for g in range(4):
                dc = grp * 4 + g
                tw(bank, lambda e, dc=dc, g=g, pst=pst: e.transpose(
                    pst[:, g * 128:(g + 1) * 128],
                    LNBUF[:, dc * 128:(dc + 1) * 128],
                    IDEN[:]), [o_xn, d_iden])
            t_ops.append(prd("vector", bank, lambda e, grp=grp, pst=pst,
                             xtb=xtb: e.tensor_copy(
                xtb[:, grp * 512:(grp + 1) * 512], pst[:])))
        prev_tops = t_ops
        slot = j % 4
        for dc in range(DC):
            tw("psS1", lambda e, dc=dc, slot=slot, xtb=xtb: e.matmul(
                psS1[0:64, slot * 128:(slot + 1) * 128],
                WQT[:, dc * 64:(dc + 1) * 64],
                xtb[:, dc * 128:(dc + 1) * 128],
                start=(dc == 0), stop=(dc == DC - 1)),
                (t_ops + [o_wqt]) if dc == 0 else [])
        if slot == 3:
            g = j // 4
            p1 = prd("scalar", "psS1", lambda e: e.activation(
                TQ1[:], psS1[0:64, :], AF.Relu))
            p2 = prd("vector", "psS1", lambda e: e.tensor_single_scalar(
                TQ2[:], psS1[0:64, :], 0.0, ALU.min))
            p3 = S(lambda e: e.activation(TQ2[:], TQ2[:], AF.Exp), [p2])
            p4 = V(lambda e, g=g: e.tensor_tensor(
                QT[:, g * 512:(g + 1) * 512], TQ1[:], TQ2[:], ALU.add),
                [p1, p3])
            phi_done.append(p4)

    PHASE_MARKS["q_done_mark"] = len(P.ops)
    sum_ops3 = []
    att_done = {}

    def p5_stage1(j):
        g = j // 4
        par = _state["emit_i"] % 2
        _state["emit_i"] += 1
        xh = XTH if par == 0 else XTH2
        xl = XTL if par == 0 else XTL2
        bk = (("psA", "psB") if par == 0 else ("psC", "psD"))
        for oh in (0, 1):
            bank = bk[oh]
            pbank = BANKS[bank]
            for g2 in range(4):
                dc = oh * 4 + g2
                tw(bank, lambda e, j=j, dc=dc, g2=g2, pbank=pbank: e.matmul(
                    pbank[:, g2 * 128:(g2 + 1) * 128],
                    MSB[0:64, dc * 128:(dc + 1) * 128],
                    QT[:, j * 128:(j + 1) * 128],
                    start=(g2 == 0), stop=(g2 == 3)),
                    ([phi_done[g]] + m_done) if g2 == 0 else [])
        t_ops = []
        for oh in (0, 1):
            bank = bk[oh]
            pbank = BANKS[bank]
            cs = slice(oh * 512, oh * 512 + 512)
            e1 = prd("vector", bank, lambda e, cs=cs, pbank=pbank, xh=xh:
                     e.tensor_copy(xh[:, cs], pbank[:, 0:512]))
            e2 = prd("scalar", bank, lambda e, cs=cs, pbank=pbank:
                     e.activation(SQS[:, cs], pbank[:, 0:512], AF.Identity),
                     [e1])
            e3 = V(lambda e, cs=cs, xh=xh, xl=xl: e.tensor_sub(
                xl[:, cs], SQS[:, cs], xh[:, cs]), [e2, e1])
            t_ops += [e1, e3]
        for oh in (0, 1):
            bank = bk[oh]
            pbank = BANKS[bank]
            for i3, (xb, wb) in enumerate(((xh, WXTH), (xh, WXTL),
                                           (xl, WXTH))):
                for dc in range(DC):
                    tw(bank, lambda e, dc=dc, oh=oh, pbank=pbank, xb=xb,
                       wb=wb, i3=i3: e.matmul(
                        pbank[:, 0:512], xb[:, dc * 128:(dc + 1) * 128],
                        wb[:, dc * D + oh * 512:dc * D + oh * 512 + 512],
                        start=(dc == 0 and i3 == 0),
                        stop=(dc == DC - 1 and i3 == 2)),
                        (t_ops + wo_evacs) if dc == 0 and i3 == 0 else [])

        def fin():
            r1 = V(lambda e, j=j: e.tensor_scalar(
                XRES[:, j * D:(j + 1) * D], XRES[:, j * D:(j + 1) * D],
                st["opr"][:, j:j + 1], st["mr2"][:, j:j + 1],
                ALU.mult, ALU.subtract), stats2 + ffn1_done[j])
            res_ops = []
            for oh in (0, 1):
                bank = bk[oh]
                pbank = BANKS[bank]
                res_ops.append(prd("vector", bank,
                                   lambda e, j=j, oh=oh, pbank=pbank:
                                   e.tensor_tensor(
                    XRES[:, j * D + oh * 512:j * D + oh * 512 + 512],
                    XRES[:, j * D + oh * 512:j * D + oh * 512 + 512],
                    pbank[:, 0:512], ALU.add), [r1]))
            att_done[j] = res_ops
            sum_ops3.append(V(lambda e, j=j: e.reduce_sum(
                st["s3"][:, j:j + 1], XRES[:, j * D:(j + 1) * D], axis=AX.X),
                res_ops))
            sum_ops3.append(S(lambda e, j=j: e.activation(
                SQS[:], XRES[:, j * D:(j + 1) * D], AF.Square,
                accum_out=st["q3"][:, j:j + 1]), res_ops))
        return fin

    p5_prev = None
    for j in range(TT):
        f = p5_stage1(j)
        if p5_prev is not None:
            p5_prev()
        p5_prev = f
    p5_prev()

    stats3 = batch_stats(st["s3"], st["q3"], st["m3"], st["r3"], st["nb3"],
                         sum_ops3)

    PHASE_MARKS["att_done_mark"] = len(P.ops)
    _state["prev"] = prev_tops
    store_ops = []
    fin_prev = None
    for j in range(TT):
        fin = emit_ffn_tile(j, st["r3"], st["nb3"],
                            stats3 + att_done[j], None, None, False)
        if fin_prev is not None:
            jp, fp = fin_prev
            store_ops.append(STD(lambda e, jp=jp: e.dma_start(
                out=ys[jp * 128:(jp + 1) * 128, :],
                in_=XRES[:, jp * D:(jp + 1) * D]), fp()))
        fin_prev = (j, fin)
    jp, fp = fin_prev
    store_ops.append(STD(lambda e, jp=jp: e.dma_start(
        out=ys[jp * 128:(jp + 1) * 128, :],
        in_=XRES[:, jp * D:(jp + 1) * D]), fp()))

    PHASE_MARKS["ffn2_done_mark"] = len(P.ops)
    P.add("gpsimd", lambda e: e.nop(), store_ops)

    if limit is not None:
        P.ops = P.ops[:limit]
        if DUMP_AT_LIMIT:
            deps = list(range(max(0, limit - 60), limit))
            for nm in DUMP_AT_LIMIT:
                ap = BUFS[nm]
                shp = list(ap.tensor.shape)
                dt = ap.tensor.dtype
                dst = nc.dram_tensor("dump_" + nm, shp, dt,
                                     kind="ExternalOutput").ap()
                P.add("gpsimd", lambda e, ap=ap, dst=dst: e.dma_start(
                    out=dst, in_=ap), deps, dma=True)
            P.add("gpsimd", lambda e: e.nop(),
                  list(range(len(P.ops)))[-len(DUMP_AT_LIMIT):])
    P.finalize()
    with ExitStack() as es:
        sems = {}
        for name in ("tensor", "vector", "scalar", "sync", "gpsimd"):
            sems[name] = es.enter_context(nc.semaphore("sem_" + name))
        dma_sems = [es.enter_context(nc.semaphore("dsem_%d" % i))
                    for i in range(P.n_dma)]
        block = es.enter_context(nc.Block())

        @block.tensor
        def _(eng):
            P.replay("tensor", eng, sems, dma_sems)

        @block.vector
        def _(eng):
            P.replay("vector", eng, sems, dma_sems)

        @block.scalar
        def _(eng):
            P.replay("scalar", eng, sems, dma_sems)

        @block.sync
        def _(eng):
            P.replay("sync", eng, sems, dma_sems)

        @block.gpsimd
        def _(eng):
            P.replay("gpsimd", eng, sems, dma_sems)
    return nc


_NC_CACHE = None


def kernel(**inputs):
    global _NC_CACHE
    f32 = np.float32
    x = np.ascontiguousarray(inputs["L0_attn_x"], dtype=f32)
    nw = np.ascontiguousarray(inputs["L0_attn_norm_weight"],
                              dtype=f32).reshape(1, D)
    wk = np.ascontiguousarray(inputs["L0_attn_Wk"], dtype=f32)
    wv = np.ascontiguousarray(inputs["L0_attn_Wv"], dtype=f32)
    wq = np.ascontiguousarray(inputs["L0_attn_Wq"], dtype=f32)
    eta = np.ascontiguousarray(inputs["L0_attn_eta"],
                               dtype=f32).reshape(1, 64)
    wo = np.ascontiguousarray(inputs["L0_attn_Wout"], dtype=f32)
    wf = np.ascontiguousarray(inputs["L0_ffn_weight"], dtype=f32)
    alphac = np.ascontiguousarray(
        np.broadcast_to(np.asarray(inputs["L0_ffn_alpha"],
                                   dtype=f32).reshape(1, 1), (128, 1)))
    bias = np.ascontiguousarray(inputs["L0_ffn_bias"],
                                dtype=f32).reshape(1, D)

    iden = np.eye(128, dtype=f32)
    ii, jj = np.indices((64, 64))
    um01 = (jj > ii).astype(f32)
    umb = np.where(jj > ii, 0.0, -1e30).astype(f32)

    B, T = x.shape[0], x.shape[1]
    in_maps = []
    for core in range(8):
        b, h = core // 2, core % 2
        in_maps.append({
            "xs": np.ascontiguousarray(x[b, h * NTOK:(h + 1) * NTOK, :]),
            "xt": np.ascontiguousarray(x[b, T - TAILC:, :]),
            "wf": wf, "wv": wv, "wo": wo, "wk": wk, "wq": wq,
            "nwr": nw, "biasr": bias, "alphar": alphac, "etar": eta,
            "idenr": iden, "um01r": um01, "umbr": umb,
        })
    if _NC_CACHE is None:
        _NC_CACHE = build_nc()
    res = run_bass_kernel_spmd(_NC_CACHE, in_maps, core_ids=list(range(8)))
    y = np.empty((B, T, D), dtype=f32)
    for core in range(8):
        b, h = core // 2, core % 2
        y[b, h * NTOK:(h + 1) * NTOK, :] = res.results[core]["ys"]
    return y


# version 14
# speedup vs baseline: 1.0181x; 1.0181x over previous
import numpy as np
import concourse.bass as bass
import concourse.mybir as mybir
from concourse.bass_utils import run_bass_kernel_spmd
from contextlib import ExitStack

F32 = mybir.dt.float32
BF16 = mybir.dt.bfloat16
AF = mybir.ActivationFunctionType
ALU = mybir.AluOpType
AX = mybir.AxisListType

NTOK = 2048
TT = 16
D = 1024
DC = 8
TAILC = 64
K_SWEEPS = 8
NF = 65536.0
EPS_LN = 1e-5
EPS_RMS = 1e-6
ETA_MIN = 1e-6
CLAMP = 20.0


class Prog:
    ENG = ("tensor", "vector", "scalar", "sync", "gpsimd")

    def __init__(self):
        self.ops = []

    def add(self, engine, fn, deps=(), dma=False):
        assert engine in self.ENG
        ds = sorted({d for d in deps if d is not None})
        self.ops.append({"e": engine, "fn": fn, "deps": ds, "dma": dma,
                         "need_inc": dma})
        return len(self.ops) - 1

    def finalize(self):
        for op in self.ops:
            for d in op["deps"]:
                dop = self.ops[d]
                if dop["e"] != op["e"] or dop["dma"]:
                    dop["need_inc"] = True
        cnt = {}
        self.n_dma = 0
        for op in self.ops:
            if op["dma"]:
                op["dma_idx"] = self.n_dma
                self.n_dma += 1
                op["inc_val"] = 16
            elif op["need_inc"]:
                cnt[op["e"]] = cnt.get(op["e"], 0) + 1
                op["inc_val"] = cnt[op["e"]]
            else:
                op["inc_val"] = None

    def replay(self, engine, handle, sems, dma_sems):
        hw = {}
        dma_waited = set()
        drained_at = -1
        pos = {}
        n_here = 0
        for i, op in enumerate(self.ops):
            if op["e"] != engine:
                continue
            need_drain = False
            for d in op["deps"]:
                dop = self.ops[d]
                if dop["dma"]:
                    di = dop["dma_idx"]
                    if di not in dma_waited:
                        handle.wait_ge(dma_sems[di], 16)
                        dma_waited.add(di)
                    continue
                if dop["e"] == engine:
                    if engine in ("vector", "scalar") and d in pos \
                            and pos[d] >= n_here - 3 and pos[d] > drained_at:
                        need_drain = True
                    continue
                v = dop["inc_val"]
                if v is None:
                    continue
                if hw.get(dop["e"], 0) < v:
                    handle.wait_ge(sems[dop["e"]], v)
                    hw[dop["e"]] = v
            if need_drain:
                handle.drain()
                drained_at = n_here - 1
            instr = op["fn"](handle)
            if op["dma"]:
                instr.then_inc(dma_sems[op["dma_idx"]], 16)
            elif op["need_inc"]:
                instr.then_inc(sems[engine], 1)
            pos[i] = n_here
            n_here += 1


PHASE_MARKS = {}
BUFS = {}
DUMP_AT_LIMIT = None


def build_nc(limit=None, debug_dump=False):
    nc = bass.Bass("TRN2", target_bir_lowering=False,
                   detect_race_conditions=False)
    xs = nc.dram_tensor("xs", [NTOK, D], F32, kind="ExternalInput").ap()
    xt = nc.dram_tensor("xt", [TAILC, D], F32, kind="ExternalInput").ap()
    wf = nc.dram_tensor("wf", [D, D], F32, kind="ExternalInput").ap()
    wv = nc.dram_tensor("wv", [D, D], F32, kind="ExternalInput").ap()
    wo = nc.dram_tensor("wo", [D, D], F32, kind="ExternalInput").ap()
    wk = nc.dram_tensor("wk", [64, D], F32, kind="ExternalInput").ap()
    wq = nc.dram_tensor("wq", [64, D], F32, kind="ExternalInput").ap()
    nwr = nc.dram_tensor("nwr", [1, D], F32, kind="ExternalInput").ap()
    biasr = nc.dram_tensor("biasr", [1, D], F32, kind="ExternalInput").ap()
    alphar = nc.dram_tensor("alphar", [128, 1], F32, kind="ExternalInput").ap()
    etar = nc.dram_tensor("etar", [1, 64], F32, kind="ExternalInput").ap()
    idenr = nc.dram_tensor("idenr", [128, 128], F32, kind="ExternalInput").ap()
    um01r = nc.dram_tensor("um01r", [64, 64], F32, kind="ExternalInput").ap()
    umbr = nc.dram_tensor("umbr", [64, 64], F32, kind="ExternalInput").ap()
    ys = nc.dram_tensor("ys", [NTOK, D], F32, kind="ExternalOutput").ap()

    BUFS.clear()

    def sb(name, shape, dt=F32):
        ap = nc.alloc_sbuf_tensor(name, shape, dt).ap()
        BUFS[name] = ap
        return ap

    XRES = sb("XRES", [128, TT * D])
    LNBUF = sb("LNBUF", [128, D])
    SQS = sb("SQS", [128, D])
    LNH = sb("LNH", [128, D], BF16)
    LNL = sb("LNL", [128, D], BF16)
    LNB2H = sb("LNB2H", [128, D], BF16)
    LNB2L = sb("LNB2L", [128, D], BF16)
    XTH = sb("XTH", [128, D], BF16)
    XTL = sb("XTL", [128, D], BF16)
    XTH2 = sb("XTH2", [128, D], BF16)
    XTL2 = sb("XTL2", [128, D], BF16)
    XT = sb("XT", [128, D])
    WT1 = sb("WT1", [128, DC * D], BF16)
    WXTH = sb("WXTH", [128, DC * D], BF16)
    WXTL = sb("WXTL", [128, DC * D], BF16)
    WSTAGE = sb("WSTAGE", [128, D])
    XT2 = WSTAGE
    WKTH = sb("WKTH", [128, 512], BF16)
    WKTL = sb("WKTL", [128, 512], BF16)
    WQT = sb("WQT", [128, 512])
    NWRP = sb("NWRP", [128, D])
    IDEN = sb("IDEN", [128, 128])
    IDENB = sb("IDENB", [128, 128], BF16)
    UM01 = sb("UM01", [64, 64])
    UMB = sb("UMB", [64, 64])
    ONE1 = sb("ONE1", [1, 128])
    ONES2 = sb("ONES2", [128, 128])
    HBr = sb("HBr", [1, D])
    BR = sb("BR", [1, D])
    NWrow = sb("NWrow", [1, D])
    AL = sb("AL", [128, 1])
    st = {}
    for nm in ("s1", "q1", "m1", "r1", "nb1",
               "s2", "q2", "m2", "r2", "nb2", "rrs", "opr", "mr2",
               "s3", "q3", "m3", "r3", "nb3", "tA", "tB", "tC"):
        st[nm] = sb("st_" + nm, [128, 16])
    ASUM = sb("ASUM", [128, 8])
    DLT = sb("DLT", [128, 1])
    NDLT = sb("NDLT", [128, 1])
    HA = sb("HA", [128, 1])
    TOT = sb("TOT", [1, 128])
    U8 = sb("U8", [64, NTOK])
    XTs = U8[:, 0:D]
    X1T = U8[:, D:2 * D]
    QT = U8[:, 0:NTOK]
    TMPT = sb("TMPT", [64, D])
    TMH = sb("TMH", [64, D], BF16)
    TML = sb("TML", [64, D], BF16)
    XTTH = sb("XTTH", [128, 512], BF16)
    XTTL = sb("XTTL", [128, 512], BF16)
    VSB = sb("VSB", [64, D])
    VTT = sb("VTT", [128, 512])
    MSB = sb("MSB", [64, D])
    TQ1 = sb("TQ1", [64, 512])
    TQ2 = sb("TQ2", [64, 512])
    m6 = {}
    for nm in ("ksb", "ke", "kT", "keT", "G", "G2", "H", "GU", "GHU", "G2HU",
               "ET", "DMT", "EG", "EGH", "EG2H", "ATm", "AR", "TM1", "ETARP"):
        m6[nm] = sb("m_" + nm, [64, 64])
    col_names = ("sig", "kap", "kts", "ubar", "ksg", "Hd", "G2d", "gdh",
                 "cc", "lamc", "phic", "cu", "mk", "ph", "th", "t2", "t3",
                 "t4", "q1c", "u1", "mk2", "qh", "t5", "c2", "t6", "t7", "t8",
                 "t9", "s2c", "t10", "t11", "nlc", "e1", "psi", "muc", "nms",
                 "c1t", "v1t", "m1t", "r1t", "nb1t", "c2t", "v2t", "m2t",
                 "r2t", "rst", "sb2t", "vpe", "vf", "tz1", "tz2")
    colbuf = sb("COLS", [64, 64])
    col = {nm: colbuf[:, i:i + 1] for i, nm in enumerate(col_names)}
    IDX_S2C = col_names.index("s2c")
    IDX_MUC = col_names.index("muc")
    MV = sb("MV", [64, 4])
    LNR = sb("LNR", [1, 65])
    LAMR = sb("LAMR", [1, 64])
    PHR = sb("PHR", [1, 65])
    S2R = sb("S2R", [1, 64])
    RECR = sb("RECR", [1, 64])
    ETAR = sb("ETAR", [1, 64])
    LAMC1 = sb("LAMC1", [1, 1])
    MS1 = sb("MS1", [1, 1])
    WKS = X1T
    WQS = TMPT

    def psalloc(name, shape):
        return nc.alloc_psum_tensor(name, shape, F32).ap()

    psT = psalloc("psT", [128, 512])
    psT2 = psalloc("psT2", [128, 512])
    psA = psalloc("psA", [128, 512])
    psB = psalloc("psB", [128, 512])
    psC = psalloc("psC", [128, 512])
    psD = psalloc("psD", [128, 512])
    psS1 = psalloc("psS1", [64, 512])
    psS2 = psalloc("psS2", [64, 512])
    BANKS = {"psT": psT, "psT2": psT2, "psA": psA, "psB": psB,
             "psC": psC, "psD": psD, "psS1": psS1, "psS2": psS2}

    P = Prog()
    ps_writer = {b: None for b in BANKS}
    ps_reader = {b: [] for b in BANKS}

    def tw(bank, fn, deps=()):
        d = list(deps) + [oid for (oid, _) in ps_reader[bank]]
        i = P.add("tensor", fn, d)
        ps_writer[bank] = i
        ps_reader[bank] = []
        return i

    def prd(engine, bank, fn, deps=()):
        d = (list(deps) + [ps_writer[bank]]
             + [oid for (oid, eng2) in ps_reader[bank] if eng2 != engine])
        i = P.add(engine, fn, d)
        ps_reader[bank].append((i, engine))
        return i

    def V(fn, deps=()):
        return P.add("vector", fn, deps)

    def S(fn, deps=()):
        return P.add("scalar", fn, deps)

    def LD(fn, deps=()):
        return P.add("sync", fn, deps, dma=True)

    def STD(fn, deps=()):
        return P.add("gpsimd", fn, deps, dma=True)

    d_iden = LD(lambda e: e.dma_start(out=IDEN[:], in_=idenr))
    o_idenb = V(lambda e: e.tensor_copy(IDENB[:], IDEN[:]), [d_iden])
    d_um01 = LD(lambda e: e.dma_start(out=UM01[:], in_=um01r))
    d_umb = LD(lambda e: e.dma_start(out=UMB[:], in_=umbr))
    d_nw = LD(lambda e: e.dma_start(out=NWrow[:], in_=nwr))
    d_bias = LD(lambda e: e.dma_start(out=BR[:], in_=biasr))
    d_al = LD(lambda e: e.dma_start(out=AL[:], in_=alphar))
    d_eta = LD(lambda e: e.dma_start(out=ETAR[:], in_=etar))
    d_xt = LD(lambda e: e.dma_start(out=XTs[:], in_=xt))

    o_ones1 = V(lambda e: e.memset(ONE1[:], 1.0))
    o_ones2 = V(lambda e: e.memset(ONES2[:], 1.0))
    o_z1 = V(lambda e: e.memset(LNR[:, 0:1], 0.0))
    o_z2 = V(lambda e: e.memset(PHR[:, 0:1], 0.0))
    o_z3 = V(lambda e: e.memset(colbuf[:], 0.0))
    o_zlam = V(lambda e: e.memset(LAMR[:], 0.0))

    o_hb = V(lambda e: e.tensor_scalar_mul(HBr[:], BR[:], 0.5), [d_bias])
    o_ha2 = V(lambda e: e.tensor_scalar_mul(HA[:], AL[:], 0.5), [d_al])
    o_nw1 = tw("psA", lambda e: e.matmul(psA[:, 0:512], ONE1[:],
                                         NWrow[0:1, 0:512],
                                         start=True, stop=True),
               [o_ones1, d_nw])
    o_nw1e = prd("vector", "psA", lambda e: e.tensor_copy(NWRP[:, 0:512],
                                                          psA[:, 0:512]))
    o_nw2 = tw("psB", lambda e: e.matmul(psB[:, 0:512], ONE1[:],
                                         NWrow[0:1, 512:1024],
                                         start=True, stop=True),
               [o_ones1, d_nw])
    o_nw2e = prd("vector", "psB", lambda e: e.tensor_copy(NWRP[:, 512:1024],
                                                          psB[:, 0:512]))

    def prep_weight(src_hbm, dests, scale_nw, first_dma_deps=(), split=False):
        all_evacs = []
        prev_stage_readers = list(first_dma_deps)
        sums = []
        dbuf = split != "tern"
        prev2 = {0: list(prev_stage_readers), 1: list(prev_stage_readers)}
        for r in range(8):
            stg = (XT if (dbuf and r % 2 == 1) else WSTAGE)
            dma = LD(lambda e, r=r, stg=stg: e.dma_start(
                out=stg[:],
                in_=src_hbm[r * 128:(r + 1) * 128, :]),
                prev2[r % 2 if dbuf else 0])
            base = [dma]
            if scale_nw:
                base.append(V(lambda e, stg=stg: e.tensor_tensor(
                    stg[:], stg[:], NWRP[:], ALU.mult),
                    [dma, o_nw1e, o_nw2e]))
            if src_hbm is wf:
                sums.append(S(lambda e, r=r, stg=stg: e.activation(
                    SQS[:], stg[:], AF.Abs,
                    accum_out=ASUM[:, r:r + 1]), base))
            prev_stage_readers = list(sums[-1:])
            for grp in range(2):
                bank = "psT" if grp == 0 else "psT2"
                pst = BANKS[bank]
                tids = []
                for g in range(4):
                    dc = grp * 4 + g
                    tids.append(tw(bank, lambda e, dc=dc, g=g, pst=pst,
                                   stg=stg: e.transpose(
                        pst[:, g * 128:(g + 1) * 128],
                        stg[:, dc * 128:(dc + 1) * 128],
                        IDEN[:]), base + [d_iden]))
                if split == "tern":
                    for g in range(4):
                        dc = grp * 4 + g
                        cs = slice(dc * D + r * 128, dc * D + (r + 1) * 128)
                        e1 = prd("vector", bank, lambda e, g=g, pst=pst:
                                 e.tensor_single_scalar(
                            LNBUF[:, 0:128], pst[:, g * 128:(g + 1) * 128],
                            DLT[:], ALU.is_gt), [o_dlbe])
                        e2 = prd("vector", bank, lambda e, g=g, pst=pst:
                                 e.tensor_single_scalar(
                            LNBUF[:, 128:256], pst[:, g * 128:(g + 1) * 128],
                            NDLT[:], ALU.is_lt), [o_ndl, e1])
                        e3 = V(lambda e: e.tensor_sub(
                            LNBUF[:, 0:128], LNBUF[:, 0:128],
                            LNBUF[:, 128:256]), [e2])
                        e4 = V(lambda e, cs=cs: e.tensor_scalar_mul(
                            dests[:, cs], LNBUF[:, 0:128], HA[:]),
                            [e3, o_ha2])
                        all_evacs.append(e4)
                elif split:
                    hi, lo = dests
                    for g in range(4):
                        dc = grp * 4 + g
                        cs = slice(dc * D + r * 128, dc * D + (r + 1) * 128)
                        e1 = prd("vector", bank, lambda e, cs=cs, g=g,
                                 pst=pst, hi=hi: e.tensor_copy(
                            hi[:, cs], pst[:, g * 128:(g + 1) * 128]))
                        e2 = prd("scalar", bank, lambda e, g=g, pst=pst:
                                 e.activation(SQS[:, 0:128],
                                              pst[:, g * 128:(g + 1) * 128],
                                              AF.Identity), [e1])
                        e3 = V(lambda e, cs=cs, hi=hi, lo=lo: e.tensor_sub(
                            lo[:, cs], SQS[:, 0:128], hi[:, cs]), [e2, e1])
                        all_evacs += [e1, e3]
                else:
                    dest = dests
                    for g in range(4):
                        dc = grp * 4 + g
                        cs = slice(dc * D + r * 128, dc * D + (r + 1) * 128)
                        all_evacs.append(prd(
                            "vector", bank, lambda e, cs=cs, g=g, pst=pst,
                            dest=dest: e.tensor_copy(
                                dest[:, cs], pst[:, g * 128:(g + 1) * 128])))
                prev_stage_readers += tids
                prev2[r % 2 if dbuf else 0] = list(tids)
        return all_evacs, sums

    wf_sums = []
    prev_r = {0: [], 1: []}
    for r in range(8):
        buf = XT if r % 2 == 0 else LNBUF
        dma = LD(lambda e, r=r, buf=buf: e.dma_start(
            out=buf[:], in_=wf[r * 128:(r + 1) * 128, :]), prev_r[r % 2])
        wf_sums.append(S(lambda e, r=r, buf=buf: e.activation(
            SQS[:], buf[:], AF.Abs, accum_out=ASUM[:, r:r + 1]), [dma]))
        prev_r[r % 2] = [wf_sums[-1]]

    o_as1 = V(lambda e: e.tensor_add(ASUM[:, 0:4], ASUM[:, 0:4],
                                     ASUM[:, 4:8]), wf_sums)
    o_as2 = V(lambda e: e.tensor_add(ASUM[:, 0:2], ASUM[:, 0:2],
                                     ASUM[:, 2:4]), [o_as1])
    o_as7 = V(lambda e: e.tensor_add(ASUM[:, 0:1], ASUM[:, 0:1],
                                     ASUM[:, 1:2]), [o_as2])
    o_tot = tw("psS1", lambda e: e.matmul(psS1[0:8, 0:128], ASUM[:, 0:8],
                                          ONES2[:], start=True, stop=True),
               [o_as7, o_ones2])
    o_tote = prd("vector", "psS1", lambda e: e.tensor_copy(TOT[:],
                                                           psS1[0:1, 0:128]))
    o_dl = V(lambda e: e.tensor_scalar_mul(TOT[:], TOT[:],
                                           0.7 / (1024.0 * 1024.0)), [o_tote])
    o_dlb = tw("psA", lambda e: e.matmul(psA[:, 0:128], ONE1[:], TOT[:],
                                         start=True, stop=True),
               [o_dl, o_ones1])
    o_dlbe = prd("vector", "psA", lambda e: e.tensor_copy(DLT[:],
                                                          psA[:, 0:1]))
    o_ndl = V(lambda e: e.tensor_scalar_mul(NDLT[:], DLT[:], -1.0), [o_dlbe])

    tern_ops, _ = prep_weight(wf, WT1, scale_nw=False,
                              first_dma_deps=[], split="tern")

    d_x = [STD(lambda e, j=j: e.dma_start(
        out=XRES[:, j * D:(j + 1) * D], in_=xs[j * 128:(j + 1) * 128, :]))
        for j in range(TT)]

    def batch_stats(s_t, q_t, m_t, r_t, nb_t, deps, extra=None):
        b1 = V(lambda e: e.tensor_scalar_mul(m_t[:], s_t[:], 1.0 / D), deps)
        b2 = V(lambda e: e.tensor_scalar_mul(st["tA"][:], q_t[:], 1.0 / D),
               deps)
        b3 = V(lambda e: e.tensor_tensor(st["tB"][:], m_t[:], m_t[:],
                                         ALU.mult), [b1])
        b4 = V(lambda e: e.tensor_sub(st["tA"][:], st["tA"][:], st["tB"][:]),
               [b2, b3])
        b5 = V(lambda e: e.tensor_single_scalar(st["tB"][:], st["tA"][:],
                                                EPS_LN, ALU.add), [b4])
        b6 = S(lambda e: e.activation(r_t[:], st["tB"][:], AF.Sqrt), [b5])
        b7 = V(lambda e: e.reciprocal(r_t[:], r_t[:]), [b6])
        b8 = V(lambda e: e.tensor_tensor(nb_t[:], m_t[:], r_t[:], ALU.mult),
               [b7])
        b9 = V(lambda e: e.tensor_scalar_mul(nb_t[:], nb_t[:], -1.0), [b8])
        out = [b9, b7]
        if extra == "rms2":
            c1 = V(lambda e: e.reciprocal(st["tC"][:], st["tB"][:]), [b5, b7])
            c2 = V(lambda e: e.tensor_tensor(st["tC"][:], st["tA"][:],
                                             st["tC"][:], ALU.mult), [c1])
            c3 = V(lambda e: e.tensor_single_scalar(st["tC"][:], st["tC"][:],
                                                    EPS_RMS, ALU.add), [c2])
            c4 = S(lambda e: e.activation(st["tC"][:], st["tC"][:], AF.Sqrt),
                   [c3])
            c5 = V(lambda e: e.reciprocal(st["tC"][:], st["tC"][:]), [c4])
            c6 = V(lambda e: e.tensor_tensor(st["rrs"][:], r_t[:],
                                             st["tC"][:], ALU.mult), [c5])
            c7 = V(lambda e: e.tensor_single_scalar(st["opr"][:], r_t[:], 1.0,
                                                    ALU.add), [b7])
            c8 = V(lambda e: e.tensor_tensor(st["mr2"][:], m_t[:], r_t[:],
                                             ALU.mult), [b7, b9])
            c9 = V(lambda e: e.tensor_tensor(nb_t[:], m_t[:], st["rrs"][:],
                                             ALU.mult), [c6, b9])
            c10 = V(lambda e: e.tensor_scalar_mul(nb_t[:], nb_t[:], -1.0),
                    [c9])
            out = [c6, c7, c8, c10]
        return out

    _FFN1_EMITTED = set()
    stats1 = []
    sum_ops2 = []
    ffn1_done = {}
    _state = {"prev": tern_ops[-1:], "emit_i": 0}

    wv_evacs, _ = prep_weight(wv, (WXTH, WXTL), scale_nw=True,
                              first_dma_deps=tern_ops[-8:], split=True)

    sum_ops1 = []
    for j in range(TT):
        sum_ops1.append(V(lambda e, j=j: e.reduce_sum(
            st["s1"][:, j:j + 1], XRES[:, j * D:(j + 1) * D], axis=AX.X),
            [d_x[j]]))
        sum_ops1.append(S(lambda e, j=j: e.activation(
            LNB2L[:], XRES[:, j * D:(j + 1) * D], AF.Square,
            accum_out=st["q1"][:, j:j + 1]), [d_x[j]]))
    stats1.extend(batch_stats(st["s1"], st["q1"], st["m1"], st["r1"],
                              st["nb1"], sum_ops1))


    d_wk = LD(lambda e: e.dma_start(out=WKS[:], in_=wk))
    d_wq = LD(lambda e: e.dma_start(out=WQS[:], in_=wq))
    o_wks = V(lambda e: e.tensor_tensor(WKS[:], WKS[:], NWRP[0:64, :],
                                        ALU.mult), [d_wk, o_nw1e, o_nw2e])
    o_wqs = V(lambda e: e.tensor_tensor(WQS[:], WQS[:], NWRP[0:64, :],
                                        ALU.mult), [d_wq, o_nw1e, o_nw2e])
    for dc in range(DC):
        tw("psT", lambda e, dc=dc: e.transpose(
            psT[:, dc * 64:(dc + 1) * 64],
            WKS[0:64, dc * 128:(dc + 1) * 128], IDEN[0:64, 0:64]),
            [o_wks, d_iden] if dc == 0 else [])
    o_wkh = prd("vector", "psT", lambda e: e.tensor_copy(WKTH[:], psT[:]))
    o_wkf = prd("scalar", "psT", lambda e: e.activation(SQS[:, 0:512],
                                                        psT[:], AF.Identity),
                [o_wkh])
    o_wkt = V(lambda e: e.tensor_sub(WKTL[:], SQS[:, 0:512], WKTH[:]),
              [o_wkf, o_wkh])
    for dc in range(DC):
        tw("psT2", lambda e, dc=dc: e.transpose(
            psT2[:, dc * 64:(dc + 1) * 64],
            WQS[0:64, dc * 128:(dc + 1) * 128], IDEN[0:64, 0:64]),
            [o_wqs, d_iden] if dc == 0 else [])
    o_wqt = prd("vector", "psT2", lambda e: e.tensor_copy(WQT[:], psT2[:]))

    o_et1 = S(lambda e: e.activation(ETAR[:], ETAR[:], AF.Abs), [d_eta])
    o_et2 = V(lambda e: e.tensor_single_scalar(ETAR[:], ETAR[:], ETA_MIN,
                                               ALU.max), [o_et1])
    o_etp = tw("psS2", lambda e: e.matmul(psS2[0:64, 0:64], ONE1[0:1, 0:64],
                                          ETAR[:], start=True, stop=True),
               [o_et2, o_ones1])
    o_etpe = prd("vector", "psS2", lambda e: e.tensor_copy(m6["ETARP"][:],
                                                           psS2[0:64, 0:64]))


    def emit_ffn_tile(j, r_t, nb_t, src_res, sums, sumcols, f1,
                      force_par=None):
        if force_par is None:
            par = _state["emit_i"] % 2
            _state["emit_i"] += 1
        else:
            par = force_par
            _state["emit_i"] = par + 1
        xh = XTH if par == 0 else XTH2
        xl = XTL if par == 0 else XTL2
        bk = (("psA", "psB") if par == 0 else ("psC", "psD"))
        o_h = S(lambda e, j=j: e.activation(
            LNH[:], XRES[:, j * D:(j + 1) * D], AF.Identity,
            scale=r_t[:, j:j + 1], bias=nb_t[:, j:j + 1]),
            src_res + _state["prev"])
        o_f = V(lambda e, j=j: e.tensor_scalar(
            LNBUF[:], XRES[:, j * D:(j + 1) * D], r_t[:, j:j + 1],
            nb_t[:, j:j + 1], ALU.mult, ALU.add), src_res + _state["prev"])
        o_l = V(lambda e: e.tensor_sub(LNL[:, 0:512], LNBUF[:, 0:512],
                                       LNH[:, 0:512]), [o_f, o_h])
        o_l2 = V(lambda e: e.tensor_sub(LNL[:, 512:1024], LNBUF[:, 512:1024],
                                        LNH[:, 512:1024]), [o_f, o_h])
        t_ops = []
        for bank, srcb, dstb, dep, evac_eng in (
                ("psT", LNH, xh, [o_h], "vector"),
                ("psT2", LNL, xl, [o_l, o_l2], "scalar")):
            pst = BANKS[bank].bitcast(BF16)
            for dc in range(8):
                tw(bank, lambda e, dc=dc, pst=pst, srcb=srcb: e.transpose(
                    pst[:, dc * 128:(dc + 1) * 128],
                    srcb[:, dc * 128:(dc + 1) * 128], IDENB[:]),
                    (dep + [o_idenb]) if dc == 0 else [])
            if evac_eng == "vector":
                t_ops.append(prd("vector", bank,
                                 lambda e, pst=pst, dstb=dstb: e.tensor_copy(
                    dstb[:], pst[:])))
            else:
                t_ops.append(prd("scalar", bank,
                                 lambda e, pst=pst, dstb=dstb: e.activation(
                    dstb[:], pst[:], AF.Identity)))
        mm_last = []
        for oh in (0, 1):
            bank = bk[oh]
            pbank = BANKS[bank]
            for dc in range(DC):
                for xb in (xh, xl):
                    tw(bank, lambda e, dc=dc, oh=oh, pbank=pbank, xb=xb:
                       e.matmul(
                        pbank[:, 0:512], xb[:, dc * 128:(dc + 1) * 128],
                        WT1[:, dc * D + oh * 512:dc * D + oh * 512 + 512],
                        start=(dc == 0 and xb is xh), stop=False),
                        (t_ops + tern_ops[-1:])
                        if dc == 0 and xb is xh else [])
            mm_last.append(tw(bank, lambda e, oh=oh, pbank=pbank: e.matmul(
                pbank[:, 0:512], ONE1[:], HBr[0:1, oh * 512:oh * 512 + 512],
                start=False, stop=True), [o_hb, o_ones1]))
        _state["prev"] = t_ops[-2:]

        def finish():
            res_ops = []
            for oh in (0, 1):
                bank = bk[oh]
                pbank = BANKS[bank]
                res_ops.append(prd("vector", bank,
                                   lambda e, j=j, oh=oh, pbank=pbank:
                                   e.tensor_tensor(
                    XRES[:, j * D + oh * 512:j * D + oh * 512 + 512],
                    XRES[:, j * D + oh * 512:j * D + oh * 512 + 512],
                    pbank[:, 0:512], ALU.add), src_res))
            if sums is not None:
                sums.append(V(lambda e, j=j, sumcols=sumcols: e.reduce_sum(
                    sumcols[0][:, j:j + 1], XRES[:, j * D:(j + 1) * D],
                    axis=AX.X), res_ops))
                sums.append(S(lambda e, j=j, sumcols=sumcols: e.activation(
                    SQS[:], XRES[:, j * D:(j + 1) * D], AF.Square,
                    accum_out=sumcols[1][:, j:j + 1]), res_ops))
            return res_ops
        return finish

    _fin_q = []

    def flush_fin():
        while _fin_q:
            j, fin = _fin_q.pop(0)
            ffn1_done[j] = fin()

    def emit_ffn1_tile(j, force_par=None):
        fin = emit_ffn_tile(j, st["r1"], st["nb1"],
                            stats1 + [d_x[j]], sum_ops2,
                            (st["s2"], st["q2"]), True, force_par=force_par)
        _fin_q.append((j, fin))

    PHASE_MARKS["p0_done"] = len(P.ops)
    tail_start = [d_xt, o_wkt, o_wqt, o_wkh]

    def ln_stats_small(x_ap, sum_c, var_c, m_c, r_c, nb_c, deps, rms_rs=None):
        o1 = V(lambda e: e.reduce_sum(sum_c, x_ap, axis=AX.X), deps)
        o2 = S(lambda e: e.activation(TMPT[:], x_ap, AF.Square,
                                      accum_out=var_c), deps)
        o3 = V(lambda e: e.tensor_scalar_mul(m_c, sum_c, 1.0 / D), [o1])
        o4 = V(lambda e: e.tensor_scalar_mul(var_c, var_c, 1.0 / D), [o2])
        o5 = V(lambda e: e.tensor_tensor(nb_c, m_c, m_c, ALU.mult), [o3])
        o6 = V(lambda e: e.tensor_sub(var_c, var_c, nb_c), [o4, o5])
        o7 = V(lambda e: e.tensor_single_scalar(nb_c, var_c, EPS_LN, ALU.add),
               [o6])
        o8 = S(lambda e: e.activation(r_c, nb_c, AF.Sqrt), [o7])
        o9 = V(lambda e: e.reciprocal(r_c, r_c), [o8])
        ops = [o9]
        if rms_rs is not None:
            rs_c, vpe_tmp, vf_tmp = rms_rs
            r1_ = V(lambda e: e.reciprocal(vpe_tmp, nb_c), [o7, o9])
            r2_ = V(lambda e: e.tensor_tensor(vf_tmp, var_c, vpe_tmp,
                                              ALU.mult), [r1_])
            r3_ = V(lambda e: e.tensor_single_scalar(vf_tmp, vf_tmp, EPS_RMS,
                                                     ALU.add), [r2_])
            r4_ = S(lambda e: e.activation(rs_c, vf_tmp, AF.Sqrt), [r3_])
            r5_ = V(lambda e: e.reciprocal(rs_c, rs_c), [r4_])
            ops.append(r5_)
        o10 = V(lambda e: e.tensor_tensor(nb_c, m_c, r_c, ALU.mult), [o9])
        o11 = V(lambda e: e.tensor_scalar_mul(nb_c, nb_c, -1.0), [o10])
        return ops + [o11]

    def tail_split_transpose(src64, scale_c, bias_c, deps):
        o_h = S(lambda e: e.activation(TMH[:], src64, AF.Identity,
                                       scale=scale_c, bias=bias_c), deps)
        o_f = V(lambda e: e.tensor_scalar(TMPT[:], src64, scale_c, bias_c,
                                          ALU.mult, ALU.add), deps)
        o_l = V(lambda e: e.tensor_sub(TML[:, 0:512], TMPT[:, 0:512],
                                       TMH[:, 0:512]), [o_f, o_h])
        o_l2 = V(lambda e: e.tensor_sub(TML[:, 512:1024], TMPT[:, 512:1024],
                                        TMH[:, 512:1024]), [o_f, o_h])
        outs = []
        for bank, srcb, dstb, dep in (("psT", TMH, XTTH, [o_h]),
                                      ("psT2", TML, XTTL, [o_l, o_l2])):
            pst = BANKS[bank].bitcast(BF16)
            for dc in range(8):
                tw(bank, lambda e, dc=dc, pst=pst, srcb=srcb: e.transpose(
                    pst[:, dc * 64:(dc + 1) * 64],
                    srcb[:, dc * 128:(dc + 1) * 128], IDENB[0:64, 0:64]),
                    (dep + [o_idenb]) if dc == 0 else [])
            outs.append(prd("vector", bank, lambda e, pst=pst, dstb=dstb:
                            e.tensor_copy(dstb[:], pst[:, 0:512])))
        return outs

    s_ln1t = ln_stats_small(XTs[:], col["c1t"], col["v1t"], col["m1t"],
                            col["r1t"], col["nb1t"], tail_start)
    t_lnT = tail_split_transpose(XTs[:], col["r1t"], col["nb1t"],
                                 s_ln1t + tail_start)
    x1t_done = []
    for oh, bank in ((0, "psA"), (1, "psB")):
        pbank = BANKS[bank]
        for dc in range(DC):
            for xb in (XTTH, XTTL):
                tw(bank, lambda e, dc=dc, oh=oh, pbank=pbank, xb=xb: e.matmul(
                    pbank[0:64, 0:512], xb[:, dc * 64:(dc + 1) * 64],
                    WT1[:, dc * D + oh * 512:dc * D + oh * 512 + 512],
                    start=(dc == 0 and xb is XTTH), stop=False),
                    (t_lnT + tern_ops) if dc == 0 and xb is XTTH else [])
        tw(bank, lambda e, oh=oh, pbank=pbank: e.matmul(
            pbank[0:64, 0:512], ONE1[0:1, 0:64],
            HBr[0:1, oh * 512:oh * 512 + 512],
            start=False, stop=True), [o_hb, o_ones1])
        x1t_done.append(prd("vector", bank,
                            lambda e, oh=oh, pbank=pbank: e.tensor_tensor(
            X1T[:, oh * 512:oh * 512 + 512], XTs[:, oh * 512:oh * 512 + 512],
            pbank[0:64, 0:512], ALU.add), tail_start))
    s_ln2t = ln_stats_small(X1T[:], col["c2t"], col["v2t"], col["m2t"],
                            col["r2t"], col["sb2t"], x1t_done,
                            rms_rs=(col["rst"], col["vpe"], col["vf"]))
    o_sc1 = V(lambda e: e.tensor_tensor(col["tz1"], col["r2t"], col["rst"],
                                        ALU.mult), s_ln2t)
    o_sc2 = V(lambda e: e.tensor_tensor(col["tz2"], col["m2t"], col["tz1"],
                                        ALU.mult), [o_sc1])
    o_sc3 = V(lambda e: e.tensor_scalar_mul(col["tz2"], col["tz2"], -1.0),
              [o_sc2])
    t_xnT = tail_split_transpose(X1T[:], col["tz1"], col["tz2"],
                                 [o_sc3] + t_lnT)
    for i3, (xb, wb) in enumerate(((XTTH, WKTH), (XTTH, WKTL),
                                   (XTTL, WKTH))):
        for dc in range(DC):
            tw("psS2", lambda e, dc=dc, xb=xb, wb=wb, i3=i3: e.matmul(
                psS2[0:64, 64:128], xb[:, dc * 64:(dc + 1) * 64],
                wb[:, dc * 64:(dc + 1) * 64],
                start=(dc == 0 and i3 == 0),
                stop=(dc == DC - 1 and i3 == 2)),
                (t_xnT + [o_wkt, o_wkh]) if dc == 0 and i3 == 0 else [])
    o_kr = prd("scalar", "psS2", lambda e: e.activation(TQ1[:, 0:64],
                                                        psS2[0:64, 64:128],
                                                        AF.Relu))
    o_km = prd("vector", "psS2", lambda e: e.tensor_single_scalar(
        TQ2[:, 0:64], psS2[0:64, 64:128], 0.0, ALU.min))
    o_kx = S(lambda e: e.activation(TQ2[:, 0:64], TQ2[:, 0:64], AF.Exp),
             [o_km])
    o_k = V(lambda e: e.tensor_tensor(m6["ksb"][:], TQ1[:, 0:64],
                                      TQ2[:, 0:64], ALU.add), [o_kr, o_kx])
    v_done = []
    for oh, bank in ((0, "psA"), (1, "psB")):
        pbank = BANKS[bank]
        for i3, (xb, wb) in enumerate(((XTTH, WXTH), (XTTH, WXTL),
                                       (XTTL, WXTH))):
            for dc in range(DC):
                tw(bank, lambda e, dc=dc, oh=oh, pbank=pbank, xb=xb, wb=wb,
                   i3=i3: e.matmul(
                    pbank[0:64, 0:512], xb[:, dc * 64:(dc + 1) * 64],
                    wb[:, dc * D + oh * 512:dc * D + oh * 512 + 512],
                    start=(dc == 0 and i3 == 0),
                    stop=(dc == DC - 1 and i3 == 2)),
                    (t_xnT + wv_evacs) if dc == 0 and i3 == 0 else [])
        v_done.append(prd("vector", bank, lambda e, oh=oh, pbank=pbank:
                          e.tensor_copy(
            VSB[:, oh * 512:oh * 512 + 512], pbank[0:64, 0:512])))
    wo_evacs, _ = prep_weight(wo, (WXTH, WXTL), scale_nw=False,
                              first_dma_deps=v_done, split=True)


    o_kee = V(lambda e: e.tensor_tensor(m6["ke"][:], m6["ksb"][:],
                                        m6["ETARP"][:], ALU.mult),
              [o_k, o_etpe])
    o_ktT = tw("psS2", lambda e: e.transpose(psS2[0:64, 128:192],
                                             m6["ksb"][:], IDEN[0:64, 0:64]),
               [o_k, d_iden])
    o_ktTe = prd("vector", "psS2", lambda e: e.tensor_copy(
        m6["kT"][:], psS2[0:64, 128:192]))
    o_keT = tw("psS2", lambda e: e.transpose(psS2[0:64, 192:256],
                                             m6["ke"][:], IDEN[0:64, 0:64]),
               [o_kee, d_iden])
    o_keTe = prd("vector", "psS2", lambda e: e.tensor_copy(
        m6["keT"][:], psS2[0:64, 192:256]))
    o_G = tw("psS1", lambda e: e.matmul(psS1[0:64, 0:64], m6["keT"][:],
                                        m6["kT"][:], start=True, stop=True),
             [o_ktTe, o_keTe])
    o_Ge = prd("vector", "psS1", lambda e: e.tensor_copy(m6["G"][:],
                                                         psS1[0:64, 0:64]))
    o_G2 = tw("psS1", lambda e: e.matmul(psS1[0:64, 64:128], m6["keT"][:],
                                         m6["keT"][:], start=True, stop=True),
              [o_keTe])
    o_G2e = prd("vector", "psS1", lambda e: e.tensor_copy(m6["G2"][:],
                                                          psS1[0:64, 64:128]))
    for dc in range(8):
        tw("psT", lambda e, dc=dc: e.transpose(
            psT[:, dc * 64:(dc + 1) * 64],
            VSB[:, dc * 128:(dc + 1) * 128], IDEN[0:64, 0:64]),
            (v_done + [d_iden]) if dc == 0 else [])
    t_v = [prd("vector", "psT", lambda e: e.tensor_copy(VTT[:], psT[:]))]
    for dc in range(DC):
        tw("psS1", lambda e, dc=dc: e.matmul(
            psS1[0:64, 128:192], VTT[:, dc * 64:(dc + 1) * 64],
            VTT[:, dc * 64:(dc + 1) * 64],
            start=(dc == 0), stop=(dc == DC - 1)), t_v if dc == 0 else [])
    o_He = prd("vector", "psS1", lambda e: e.tensor_copy(m6["H"][:],
                                                         psS1[0:64, 128:192]))
    o_sig = V(lambda e: e.reduce_sum(col["sig"], VSB[:], axis=AX.X), v_done)
    o_kap = V(lambda e: e.reduce_sum(col["kap"], m6["ksb"][:], axis=AX.X),
              [o_k])
    o_kpt = V(lambda e: e.reduce_sum(col["tz1"], m6["ke"][:], axis=AX.X),
              [o_kee, o_sc1])
    o_kts = V(lambda e: e.tensor_tensor(col["kts"], col["tz1"], col["sig"],
                                        ALU.mult), [o_kpt, o_sig])
    o_ub = V(lambda e: e.tensor_scalar_mul(col["ubar"], col["kts"],
                                           1.0 / NF), [o_kts])
    o_ksg = V(lambda e: e.tensor_tensor(col["ksg"], col["kap"], col["sig"],
                                        ALU.mult), [o_kap, o_sig])
    o_hd1 = V(lambda e: e.tensor_tensor(TQ1[:, 0:64], m6["H"][:],
                                        IDEN[0:64, 0:64], ALU.mult),
              [o_He, d_iden, o_k])
    o_hd = V(lambda e: e.reduce_sum(col["Hd"], TQ1[:, 0:64], axis=AX.X),
             [o_hd1])
    o_gd1 = V(lambda e: e.tensor_tensor(TQ2[:, 0:64], m6["G2"][:],
                                        IDEN[0:64, 0:64], ALU.mult),
              [o_G2e, d_iden, o_k])
    o_gd = V(lambda e: e.reduce_sum(col["G2d"], TQ2[:, 0:64], axis=AX.X),
             [o_gd1])
    o_gdh = V(lambda e: e.tensor_tensor(col["gdh"], col["G2d"], col["Hd"],
                                        ALU.mult), [o_gd, o_hd])
    o_gu = V(lambda e: e.tensor_tensor(m6["GU"][:], m6["G"][:], UM01[:],
                                       ALU.mult), [o_Ge, d_um01])
    o_gh = V(lambda e: e.tensor_tensor(m6["GHU"][:], m6["G"][:], m6["H"][:],
                                       ALU.mult), [o_Ge, o_He])
    o_ghu = V(lambda e: e.tensor_tensor(m6["GHU"][:], m6["GHU"][:], UM01[:],
                                        ALU.mult), [o_gh, d_um01])
    o_g2h = V(lambda e: e.tensor_tensor(m6["G2HU"][:], m6["G2"][:],
                                        m6["H"][:], ALU.mult), [o_G2e, o_He])
    o_g2hu = V(lambda e: e.tensor_tensor(m6["G2HU"][:], m6["G2HU"][:],
                                         UM01[:], ALU.mult), [o_g2h, d_um01])

    PHASE_MARKS["tail_done"] = len(P.ops)
    scan_pre = [o_ghu, o_g2hu, o_gu, o_ub, o_ksg, o_gdh, o_z1, o_z2, o_z3,
                o_zlam, d_umb, o_He, o_hd, o_etpe]
    prev_sweep = list(scan_pre)
    for sw in range(K_SWEEPS):
        t1 = tw("psS1", lambda e: e.matmul(psS1[0:64, 256:320],
                                           ONE1[0:1, 0:64], LAMR[:],
                                           start=True, stop=True),
                prev_sweep + [o_ones1, o_zlam])
        v1 = prd("vector", "psS1", lambda e: e.tensor_scalar(
            m6["DMT"][:], psS1[0:64, 256:320], -1.0, col["lamc"],
            ALU.mult, ALU.add), prev_sweep)
        v2 = V(lambda e: e.tensor_single_scalar(m6["DMT"][:], m6["DMT"][:],
                                                CLAMP, ALU.min), [v1])
        v3 = V(lambda e: e.tensor_tensor(m6["DMT"][:], m6["DMT"][:], UMB[:],
                                         ALU.add), [v2, d_umb])
        s1 = S(lambda e: e.activation(m6["ET"][:], m6["DMT"][:], AF.Exp),
               [v3])
        v4 = V(lambda e: e.tensor_tensor(m6["EG"][:], m6["ET"][:],
                                         m6["GU"][:], ALU.mult), [s1, o_gu])
        v5 = V(lambda e: e.tensor_tensor(m6["EGH"][:], m6["ET"][:],
                                         m6["GHU"][:], ALU.mult), [s1, o_ghu])
        v6 = V(lambda e: e.tensor_tensor(m6["EG2H"][:], m6["ET"][:],
                                         m6["G2HU"][:], ALU.mult),
               [s1, o_g2hu])
        v7 = V(lambda e: e.tensor_tensor(col["cu"], col["cc"], col["ubar"],
                                         ALU.mult), prev_sweep + [o_ub])
        v8 = V(lambda e: e.tensor_scalar_mul(m6["ATm"][:], m6["EG"][:],
                                             col["cc"]), [v4] + prev_sweep)
        t2 = tw("psS2", lambda e: e.matmul(psS2[0:64, 0:1], m6["ET"][:],
                                           col["cu"], start=True, stop=True),
                [s1, v7])
        t3 = tw("psS2", lambda e: e.matmul(psS2[0:64, 1:2], m6["EGH"][:],
                                           col["cc"], start=True, stop=True),
                [v5] + prev_sweep)
        t4 = tw("psS2", lambda e: e.matmul(psS2[0:64, 2:3], m6["EG2H"][:],
                                           col["cc"], start=True, stop=True),
                [v6] + prev_sweep)
        t5 = tw("psS2", lambda e: e.matmul(psS2[0:64, 3:4], m6["ATm"][:],
                                           col["sig"], start=True, stop=True),
                [v8, o_sig])
        t6 = tw("psS1", lambda e: e.matmul(psS1[0:64, 320:384], m6["ATm"][:],
                                           m6["H"][:], start=True, stop=True),
                [v8, o_He])
        t7 = tw("psS1", lambda e: e.transpose(psS1[0:64, 384:448],
                                              m6["ATm"][:], IDEN[0:64, 0:64]),
                [v8, d_iden])
        flush_fin()
        if sw < TT and sw not in _FFN1_EMITTED:
            emit_ffn1_tile(sw)
            _FFN1_EMITTED.add(sw)
        v9 = prd("vector", "psS2", lambda e: e.tensor_copy(MV[:],
                                                           psS2[0:64, 0:4]))
        v10 = prd("vector", "psS1", lambda e: e.tensor_copy(
            m6["AR"][:], psS1[0:64, 384:448]))
        v11 = prd("vector", "psS1", lambda e: e.tensor_tensor(
            m6["TM1"][:], psS1[0:64, 320:384], m6["AR"][:], ALU.mult), [v10])
        v12 = V(lambda e: e.reduce_sum(col["q1c"], m6["TM1"][:], axis=AX.X),
                [v11])
        v13 = S(lambda e: e.activation(col["mk"], MV[:, 0:1], AF.Identity,
                                       scale=col["kap"]), [v9, o_kap])
        v14 = V(lambda e: e.tensor_tensor(col["t2"], MV[:, 0:1], col["ksg"],
                                          ALU.mult), [v9, o_ksg])
        v15 = V(lambda e: e.tensor_sub(col["ph"], MV[:, 1:2], col["t2"]),
                [v14, v9])
        v16 = S(lambda e: e.activation(col["t3"], MV[:, 0:1], AF.Identity,
                                       scale=col["kts"]), [v9, o_kts])
        v17 = V(lambda e: e.tensor_sub(col["th"], MV[:, 2:3], col["t3"]),
                [v16, v9])
        v18 = S(lambda e: e.activation(col["t4"], MV[:, 3:4], AF.Identity,
                                       scale=col["mk"]), [v13, v9])
        v19 = V(lambda e: e.scalar_tensor_tensor(col["u1"], col["t4"], -2.0,
                                                 col["q1c"], ALU.mult,
                                                 ALU.add), [v18, v12])
        v20 = S(lambda e: e.activation(col["mk2"], col["mk"], AF.Square),
                [v13])
        v21 = V(lambda e: e.scalar_tensor_tensor(col["qh"], col["mk2"],
                                                 float(D), col["u1"],
                                                 ALU.mult, ALU.add),
                [v20, v19])
        v22 = V(lambda e: e.scalar_tensor_tensor(col["t5"], col["ph"], -2.0,
                                                 col["qh"], ALU.mult,
                                                 ALU.add), [v15, v21])
        v23 = V(lambda e: e.tensor_tensor(col["c2"], col["t5"], col["Hd"],
                                          ALU.add), [v22, o_hd])
        v24 = V(lambda e: e.tensor_single_scalar(col["c2"], col["c2"], 0.0,
                                                 ALU.max), [v23])
        s2 = S(lambda e: e.activation(col["cc"], col["c2"], AF.Sqrt), [v24])
        v25 = S(lambda e: e.activation(col["t6"], col["th"], AF.Identity,
                                       scale=col["cc"]), [s2, v17])
        v26 = S(lambda e: e.activation(col["t7"], col["cc"], AF.Square),
                [s2])
        v27 = V(lambda e: e.tensor_tensor(col["t8"], col["t7"], col["gdh"],
                                          ALU.mult), [v26, o_gdh])
        v28 = V(lambda e: e.scalar_tensor_tensor(col["t9"], col["t6"], 2.0,
                                                 col["t8"], ALU.mult,
                                                 ALU.add), [v25, v27])
        v29 = V(lambda e: e.scalar_tensor_tensor(col["s2c"], col["t9"],
                                                 1.0 / NF, col["phic"],
                                                 ALU.mult, ALU.add),
                [v28] + prev_sweep)
        v30 = S(lambda e: e.activation(col["t10"], col["cc"], AF.Identity,
                                       scale=col["ubar"]), [s2, o_ub])
        v31 = S(lambda e: e.activation(col["t11"], col["t10"], AF.Square),
                [v30])
        v32 = V(lambda e: e.tensor_sub(col["s2c"], col["s2c"], col["t11"]),
                [v29, v31])
        v33 = V(lambda e: e.tensor_single_scalar(col["s2c"], col["s2c"],
                                                 EPS_LN, ALU.add), [v32])
        v34 = V(lambda e: e.tensor_single_scalar(col["s2c"], col["s2c"],
                                                 EPS_LN, ALU.max), [v33])
        v34 = V(lambda e: e.tensor_single_scalar(col["s2c"], col["s2c"],
                                                 1e12, ALU.min), [v34])
        t8 = tw("psS2", lambda e: e.transpose(
            psS2[0:16, 64:128], colbuf[:, IDX_S2C:IDX_S2C + 16],
            IDEN[0:64, 0:64]), [v34, d_iden])
        v35 = prd("vector", "psS2", lambda e: e.tensor_copy(
            S2R[:], psS2[0:1, 64:128]))
        s3 = S(lambda e: e.activation(LNR[:, 1:65], S2R[:], AF.Ln),
               [v35, o_z1])
        v36 = V(lambda e: e.tensor_scalar_mul(LNR[:, 1:65], LNR[:, 1:65],
                                              0.5), [s3])
        v37 = V(lambda e: e.tensor_tensor_scan(LAMR[:], ONE1[0:1, 0:64],
                                               LNR[:, 0:64], 0.0, ALU.mult,
                                               ALU.add), [v36, o_ones1])
        v38 = V(lambda e: e.reciprocal(RECR[:], S2R[:]), [v35])
        v39 = V(lambda e: e.tensor_scalar(PHR[:, 1:65], RECR[:], -EPS_LN,
                                          1.0, ALU.mult, ALU.add),
                [v38, o_z2])
        t9 = tw("psS2", lambda e: e.matmul(psS2[0:64, 128:192], LAMR[:],
                                           ONE1[0:1, 0:64],
                                           start=True, stop=True),
                [v37, o_ones1])
        t10 = tw("psS2", lambda e: e.matmul(psS2[0:64, 192:256],
                                            PHR[:, 0:64], ONE1[0:1, 0:64],
                                            start=True, stop=True),
                 [v39, o_ones1])
        v40 = prd("vector", "psS2", lambda e: e.tensor_copy(
            col["lamc"], psS2[0:64, 128:129]))
        v41 = prd("vector", "psS2", lambda e: e.tensor_copy(
            col["phic"], psS2[0:64, 192:193]))
        prev_sweep = [s2, v37, v40, v41, v34, v36]
        flush_fin()
        if sw + K_SWEEPS < TT and (sw + K_SWEEPS) not in _FFN1_EMITTED:
            emit_ffn1_tile(sw + K_SWEEPS)
            _FFN1_EMITTED.add(sw + K_SWEEPS)

    m_done = []

    def emit_final_m():
        PHASE_MARKS["sweeps_done"] = len(P.ops)
        o_lc1 = V(lambda e: e.tensor_tensor(LAMC1[:], LAMR[:, 63:64],
                                            LNR[:, 64:65], ALU.add), prev_sweep)
        o_lcr = V(lambda e: e.tensor_scalar_mul(RECR[:], ONE1[0:1, 0:64],
                                                LAMC1[:]), [o_lc1, o_ones1])
        o_lcb = tw("psS2", lambda e: e.matmul(psS2[0:64, 0:64], RECR[:],
                                              ONE1[0:1, 0:64],
                                              start=True, stop=True),
                   [o_lcr, o_ones1])
        o_lcbe = prd("vector", "psS2", lambda e: e.tensor_copy(
            col["nlc"], psS2[0:64, 0:1]))
        o_nlc = V(lambda e: e.tensor_scalar_mul(col["nlc"], col["nlc"], -1.0),
                  [o_lcbe])
        o_e1 = S(lambda e: e.activation(col["e1"], col["lamc"], AF.Exp,
                                        bias=col["nlc"]), [o_nlc] + prev_sweep)
        o_psi = V(lambda e: e.tensor_tensor(col["psi"], col["e1"], col["cc"],
                                            ALU.mult), [o_e1] + prev_sweep)
        o_kps = V(lambda e: e.tensor_scalar_mul(TQ1[:, 0:64], m6["ke"][:],
                                                col["psi"]),
                  [o_psi, o_kee, o_hd1])
        mhalf = []
        for oh, bank in ((0, "psA"), (1, "psB")):
            pbank = BANKS[bank]
            mhalf.append(tw(bank, lambda e, oh=oh, pbank=pbank: e.matmul(
                pbank[0:64, 0:512], TQ1[:, 0:64],
                VSB[:, oh * 512:oh * 512 + 512],
                start=True, stop=True), [o_kps] + v_done))
        o_mu = V(lambda e: e.tensor_tensor(col["muc"], col["psi"], col["ubar"],
                                           ALU.mult), [o_psi, o_ub])
        o_ms = tw("psS2", lambda e: e.transpose(
            psS2[0:16, 256:320], colbuf[:, IDX_MUC:IDX_MUC + 16],
            IDEN[0:64, 0:64]), [o_mu, d_iden])
        o_mse = prd("vector", "psS2", lambda e: e.tensor_copy(
            S2R[:], psS2[0:1, 256:320]))
        o_ms2 = V(lambda e: e.reduce_sum(MS1[:], S2R[:], axis=AX.X), [o_mse])
        o_msr = V(lambda e: e.tensor_scalar_mul(RECR[:], ONE1[0:1, 0:64],
                                                MS1[:]), [o_ms2, o_ones1])
        o_msb = tw("psS2", lambda e: e.matmul(psS2[0:64, 320:384], RECR[:],
                                              ONE1[0:1, 0:64],
                                              start=True, stop=True),
                   [o_msr, o_ones1])
        o_msbe = prd("vector", "psS2", lambda e: e.tensor_copy(
            col["nms"], psS2[0:64, 320:321]))
        o_nms = V(lambda e: e.tensor_scalar_mul(col["nms"], col["nms"], -1.0),
                  [o_msbe])
        for oh in (0, 1):
            bank = "psA" if oh == 0 else "psB"
            pbank = BANKS[bank]
            m_done.append(prd("scalar", bank, lambda e, oh=oh, pbank=pbank:
                              e.activation(
                MSB[:, oh * 512:oh * 512 + 512], pbank[0:64, 0:512], AF.Identity,
                bias=col["nms"]), [o_nms, mhalf[oh]]))


    PHASE_MARKS["m_done_mark"] = len(P.ops)
    for j in range(TT):
        if j not in _FFN1_EMITTED:
            emit_ffn1_tile(j)
            _FFN1_EMITTED.add(j)
            flush_fin()
    flush_fin()

    stats2 = batch_stats(st["s2"], st["q2"], st["m2"], st["r2"], st["nb2"],
                         sum_ops2, extra="rms2")

    PHASE_MARKS["ffn1_done_mark"] = len(P.ops)
    prev_tops = _state["prev"]
    phi_done = []
    for j in range(TT):
        if j == 6:
            emit_final_m()
        xtb = XT if j % 2 == 0 else XT2
        o_xn = S(lambda e, j=j: e.activation(
            LNBUF[:], XRES[:, j * D:(j + 1) * D], AF.Identity,
            scale=st["rrs"][:, j:j + 1], bias=st["nb2"][:, j:j + 1]),
            stats2 + ffn1_done[j] + prev_tops)
        t_ops = []
        for grp in range(2):
            bank = "psT" if grp == 0 else "psT2"
            pst = BANKS[bank]
            for g in range(4):
                dc = grp * 4 + g
                tw(bank, lambda e, dc=dc, g=g, pst=pst: e.transpose(
                    pst[:, g * 128:(g + 1) * 128],
                    LNBUF[:, dc * 128:(dc + 1) * 128],
                    IDEN[:]), [o_xn, d_iden])
            t_ops.append(prd("vector", bank, lambda e, grp=grp, pst=pst,
                             xtb=xtb: e.tensor_copy(
                xtb[:, grp * 512:(grp + 1) * 512], pst[:])))
        prev_tops = t_ops
        slot = j % 4
        for dc in range(DC):
            tw("psS1", lambda e, dc=dc, slot=slot, xtb=xtb: e.matmul(
                psS1[0:64, slot * 128:(slot + 1) * 128],
                WQT[:, dc * 64:(dc + 1) * 64],
                xtb[:, dc * 128:(dc + 1) * 128],
                start=(dc == 0), stop=(dc == DC - 1)),
                (t_ops + [o_wqt]) if dc == 0 else [])
        if slot == 3:
            g = j // 4
            p1 = prd("scalar", "psS1", lambda e: e.activation(
                TQ1[:], psS1[0:64, :], AF.Relu))
            p2 = prd("vector", "psS1", lambda e: e.tensor_single_scalar(
                TQ2[:], psS1[0:64, :], 0.0, ALU.min))
            p3 = S(lambda e: e.activation(TQ2[:], TQ2[:], AF.Exp), [p2])
            p4 = V(lambda e, g=g: e.tensor_tensor(
                QT[:, g * 512:(g + 1) * 512], TQ1[:], TQ2[:], ALU.add),
                [p1, p3])
            phi_done.append(p4)

    PHASE_MARKS["q_done_mark"] = len(P.ops)
    sum_ops3 = []
    att_done = {}

    def p5_stage1(j):
        g = j // 4
        par = _state["emit_i"] % 2
        _state["emit_i"] += 1
        xh = XTH if par == 0 else XTH2
        xl = XTL if par == 0 else XTL2
        bk = (("psA", "psB") if par == 0 else ("psC", "psD"))
        for oh in (0, 1):
            bank = bk[oh]
            pbank = BANKS[bank]
            for g2 in range(4):
                dc = oh * 4 + g2
                tw(bank, lambda e, j=j, dc=dc, g2=g2, pbank=pbank: e.matmul(
                    pbank[:, g2 * 128:(g2 + 1) * 128],
                    MSB[0:64, dc * 128:(dc + 1) * 128],
                    QT[:, j * 128:(j + 1) * 128],
                    start=(g2 == 0), stop=(g2 == 3)),
                    ([phi_done[g]] + m_done) if g2 == 0 else [])
        t_ops = []
        for oh in (0, 1):
            bank = bk[oh]
            pbank = BANKS[bank]
            cs = slice(oh * 512, oh * 512 + 512)
            e1 = prd("vector", bank, lambda e, cs=cs, pbank=pbank, xh=xh:
                     e.tensor_copy(xh[:, cs], pbank[:, 0:512]))
            e2 = prd("scalar", bank, lambda e, cs=cs, pbank=pbank:
                     e.activation(SQS[:, cs], pbank[:, 0:512], AF.Identity),
                     [e1])
            e3 = V(lambda e, cs=cs, xh=xh, xl=xl: e.tensor_sub(
                xl[:, cs], SQS[:, cs], xh[:, cs]), [e2, e1])
            t_ops += [e1, e3]
        for oh in (0, 1):
            bank = bk[oh]
            pbank = BANKS[bank]
            for i3, (xb, wb) in enumerate(((xh, WXTH), (xh, WXTL),
                                           (xl, WXTH))):
                for dc in range(DC):
                    tw(bank, lambda e, dc=dc, oh=oh, pbank=pbank, xb=xb,
                       wb=wb, i3=i3: e.matmul(
                        pbank[:, 0:512], xb[:, dc * 128:(dc + 1) * 128],
                        wb[:, dc * D + oh * 512:dc * D + oh * 512 + 512],
                        start=(dc == 0 and i3 == 0),
                        stop=(dc == DC - 1 and i3 == 2)),
                        (t_ops + wo_evacs) if dc == 0 and i3 == 0 else [])

        def fin():
            r1 = V(lambda e, j=j: e.tensor_scalar(
                XRES[:, j * D:(j + 1) * D], XRES[:, j * D:(j + 1) * D],
                st["opr"][:, j:j + 1], st["mr2"][:, j:j + 1],
                ALU.mult, ALU.subtract), stats2 + ffn1_done[j])
            res_ops = []
            for oh in (0, 1):
                bank = bk[oh]
                pbank = BANKS[bank]
                res_ops.append(prd("vector", bank,
                                   lambda e, j=j, oh=oh, pbank=pbank:
                                   e.tensor_tensor(
                    XRES[:, j * D + oh * 512:j * D + oh * 512 + 512],
                    XRES[:, j * D + oh * 512:j * D + oh * 512 + 512],
                    pbank[:, 0:512], ALU.add), [r1]))
            att_done[j] = res_ops
            sum_ops3.append(V(lambda e, j=j: e.reduce_sum(
                st["s3"][:, j:j + 1], XRES[:, j * D:(j + 1) * D], axis=AX.X),
                res_ops))
            sum_ops3.append(S(lambda e, j=j: e.activation(
                SQS[:], XRES[:, j * D:(j + 1) * D], AF.Square,
                accum_out=st["q3"][:, j:j + 1]), res_ops))
        return fin

    p5_prev = None
    for j in range(TT):
        f = p5_stage1(j)
        if p5_prev is not None:
            p5_prev()
        p5_prev = f
    p5_prev()

    stats3 = batch_stats(st["s3"], st["q3"], st["m3"], st["r3"], st["nb3"],
                         sum_ops3)

    PHASE_MARKS["att_done_mark"] = len(P.ops)
    _state["prev"] = prev_tops
    store_ops = []
    fin_prev = None
    for j in range(TT):
        fin = emit_ffn_tile(j, st["r3"], st["nb3"],
                            stats3 + att_done[j], None, None, False)
        if fin_prev is not None:
            jp, fp = fin_prev
            store_ops.append(STD(lambda e, jp=jp: e.dma_start(
                out=ys[jp * 128:(jp + 1) * 128, :],
                in_=XRES[:, jp * D:(jp + 1) * D]), fp()))
        fin_prev = (j, fin)
    jp, fp = fin_prev
    store_ops.append(STD(lambda e, jp=jp: e.dma_start(
        out=ys[jp * 128:(jp + 1) * 128, :],
        in_=XRES[:, jp * D:(jp + 1) * D]), fp()))

    PHASE_MARKS["ffn2_done_mark"] = len(P.ops)
    P.add("gpsimd", lambda e: e.nop(), store_ops)

    if limit is not None:
        P.ops = P.ops[:limit]
        if DUMP_AT_LIMIT:
            deps = list(range(max(0, limit - 60), limit))
            for nm in DUMP_AT_LIMIT:
                ap = BUFS[nm]
                shp = list(ap.tensor.shape)
                dt = ap.tensor.dtype
                dst = nc.dram_tensor("dump_" + nm, shp, dt,
                                     kind="ExternalOutput").ap()
                P.add("gpsimd", lambda e, ap=ap, dst=dst: e.dma_start(
                    out=dst, in_=ap), deps, dma=True)
            P.add("gpsimd", lambda e: e.nop(),
                  list(range(len(P.ops)))[-len(DUMP_AT_LIMIT):])
    P.finalize()
    with ExitStack() as es:
        sems = {}
        for name in ("tensor", "vector", "scalar", "sync", "gpsimd"):
            sems[name] = es.enter_context(nc.semaphore("sem_" + name))
        dma_sems = [es.enter_context(nc.semaphore("dsem_%d" % i))
                    for i in range(P.n_dma)]
        block = es.enter_context(nc.Block())

        @block.tensor
        def _(eng):
            P.replay("tensor", eng, sems, dma_sems)

        @block.vector
        def _(eng):
            P.replay("vector", eng, sems, dma_sems)

        @block.scalar
        def _(eng):
            P.replay("scalar", eng, sems, dma_sems)

        @block.sync
        def _(eng):
            P.replay("sync", eng, sems, dma_sems)

        @block.gpsimd
        def _(eng):
            P.replay("gpsimd", eng, sems, dma_sems)
    return nc


_NC_CACHE = None


def kernel(**inputs):
    global _NC_CACHE
    f32 = np.float32
    x = np.ascontiguousarray(inputs["L0_attn_x"], dtype=f32)
    nw = np.ascontiguousarray(inputs["L0_attn_norm_weight"],
                              dtype=f32).reshape(1, D)
    wk = np.ascontiguousarray(inputs["L0_attn_Wk"], dtype=f32)
    wv = np.ascontiguousarray(inputs["L0_attn_Wv"], dtype=f32)
    wq = np.ascontiguousarray(inputs["L0_attn_Wq"], dtype=f32)
    eta = np.ascontiguousarray(inputs["L0_attn_eta"],
                               dtype=f32).reshape(1, 64)
    wo = np.ascontiguousarray(inputs["L0_attn_Wout"], dtype=f32)
    wf = np.ascontiguousarray(inputs["L0_ffn_weight"], dtype=f32)
    alphac = np.ascontiguousarray(
        np.broadcast_to(np.asarray(inputs["L0_ffn_alpha"],
                                   dtype=f32).reshape(1, 1), (128, 1)))
    bias = np.ascontiguousarray(inputs["L0_ffn_bias"],
                                dtype=f32).reshape(1, D)

    iden = np.eye(128, dtype=f32)
    ii, jj = np.indices((64, 64))
    um01 = (jj > ii).astype(f32)
    umb = np.where(jj > ii, 0.0, -1e30).astype(f32)

    B, T = x.shape[0], x.shape[1]
    in_maps = []
    for core in range(8):
        b, h = core // 2, core % 2
        in_maps.append({
            "xs": np.ascontiguousarray(x[b, h * NTOK:(h + 1) * NTOK, :]),
            "xt": np.ascontiguousarray(x[b, T - TAILC:, :]),
            "wf": wf, "wv": wv, "wo": wo, "wk": wk, "wq": wq,
            "nwr": nw, "biasr": bias, "alphar": alphac, "etar": eta,
            "idenr": iden, "um01r": um01, "umbr": umb,
        })
    if _NC_CACHE is None:
        _NC_CACHE = build_nc()
    res = run_bass_kernel_spmd(_NC_CACHE, in_maps, core_ids=list(range(8)))
    y = np.empty((B, T, D), dtype=f32)
    for core in range(8):
        b, h = core // 2, core % 2
        y[b, h * NTOK:(h + 1) * NTOK, :] = res.results[core]["ys"]
    return y
